# version 13
# baseline (speedup 1.0000x reference)
import numpy as np
import ml_dtypes
from contextlib import ExitStack

import concourse.bass as bass
import concourse.mybir as mybir
from concourse.bass_utils import run_bass_kernel_spmd

F32 = mybir.dt.float32
BF16 = mybir.dt.bfloat16
AF = mybir.ActivationFunctionType
ALU = mybir.AluOpType

D = 1024
TPC = 2048
H = 16
HD = 64
CH = 128
NCH = TPC // CH
IC = D // 128
DH = 4096
HC = DH // 128
NSL = 4
TS = TPC // NSL
EPS_ATT = 1e-6
EPS_LN = 1e-5


def soff(h):
    return 260 * ((h // 2) % 2) + 65 * (h // 4)


class Sem:
    def __init__(self, handle, name, by_order=False):
        self.h = handle
        self.name = name
        self.n = 0
        self.by_order = by_order


class Op:
    __slots__ = ("eng", "fn", "waits", "incs")

    def __init__(self, eng, fn, waits, incs):
        self.eng = eng
        self.fn = fn
        self.waits = waits
        self.incs = incs


class Prog:
    def __init__(self):
        self.ops = []

    def add(self, eng, fn, deps=(), inc=None, inck=1):
        waits = [d for d in deps if d is not None]
        incs = []
        tok = None
        if inc is not None:
            inc.n += inck
            incs.append((inc, inck))
            tok = (inc, inc.n)
        self.ops.append(Op(eng, fn, waits, incs))
        return tok

    def flush(self, block):
        per = {}
        for op in self.ops:
            per.setdefault(op.eng, []).append(op)
        table = {"tensor": block.tensor, "vector": block.vector,
                 "scalar": block.scalar, "sync": block.sync, "gpsimd": block.gpsimd}
        for name, ops in per.items():
            def body(eng, ops=ops):
                done = {}
                for op in ops:
                    for s, val in op.waits:
                        if done.get(id(s), -1) >= val:
                            continue
                        eng.wait_ge(s.h, val)
                        done[id(s)] = val
                    instr = op.fn(eng)
                    for s, k in op.incs:
                        instr.then_inc(s.h, k)
                        if s.by_order:
                            done[id(s)] = max(done.get(id(s), -1), s.n)
            table[name](body)
        self.ops = []


def build_nc(stage=3, pb_parts=3, pc=6, dbg=False):
    nc = bass.Bass("TRN2", target_bir_lowering=False)
    st = ExitStack()
    ap = lambda t: t.ap()

    x_d = ap(nc.dram_tensor("x", [D, TPC], F32, kind="ExternalInput"))
    y_d = ap(nc.dram_tensor("y", [D, TPC], F32, kind="ExternalOutput"))
    wq_d = ap(nc.dram_tensor("wq", [D, D], F32, kind="ExternalInput"))
    wk_d = ap(nc.dram_tensor("wk", [D, D], F32, kind="ExternalInput"))
    wv_d = ap(nc.dram_tensor("wv", [D, D], F32, kind="ExternalInput"))
    wbo_d = ap(nc.dram_tensor("wbo", [8, 128, 8, 128], F32, kind="ExternalInput"))
    wb1_d = ap(nc.dram_tensor("wb1", [32, 128, 8, 128], F32, kind="ExternalInput"))
    wb2_d = ap(nc.dram_tensor("wb2", [32, 128, 8, 128], F32, kind="ExternalInput"))
    wbh1_d = ap(nc.dram_tensor("wbh1", [32, 128, 32, 128], F32, kind="ExternalInput"))
    wbh2_d = ap(nc.dram_tensor("wbh2", [32, 128, 32, 128], F32, kind="ExternalInput"))
    wbw3_d = ap(nc.dram_tensor("wbw3", [8, 128, 32, 128], F32, kind="ExternalInput"))
    scl_d = ap(nc.dram_tensor("scl", [128, 136], F32, kind="ExternalInput"))
    eye_d = ap(nc.dram_tensor("eye_bf", [128, 128], BF16, kind="ExternalInput"))
    triu_d = ap(nc.dram_tensor("triu_f", [128, 128], BF16, kind="ExternalInput"))
    onesc_d = ap(nc.dram_tensor("ones_bf", [128, 1], BF16, kind="ExternalInput"))
    onesr_d = ap(nc.dram_tensor("onesr_f", [1, 128], F32, kind="ExternalInput"))
    pmask_d = ap(nc.dram_tensor("pmask", [128, 4], F32, kind="ExternalInput"))

    wscr = {k: ap(nc.dram_tensor(f"wscr_{k}", shp, BF16)) for k, shp in [
        ("wo", [8, 128, 8, 128]), ("w1", [32, 128, 8, 128]), ("w2", [32, 128, 8, 128]),
        ("h1", [32, 128, 32, 128]), ("h2", [32, 128, 32, 128]), ("w3", [8, 128, 32, 128])]}
    wsrc = {"wo": wbo_d, "w1": wb1_d, "w2": wb2_d, "h1": wbh1_d, "h2": wbh2_d, "w3": wbw3_d}
    dbg_d = ap(nc.dram_tensor("dbg", [128, 3072], BF16, kind="ExternalOutput")) if dbg else None
    cc_in = ap(nc.dram_tensor("cc_in", [128, 520], F32))
    cc_out = ap(nc.dram_tensor("cc_out", [512, 520], F32))
    rows_d = ap(nc.dram_tensor("rows_scr", [2, TPC], F32))

    def sem(name, by_order=False):
        return Sem(st.enter_context(nc.semaphore(name)), name, by_order)

    s_pe = sem("s_pe", True)
    s_dve = sem("s_dve")
    s_act = sem("s_act")
    s_pool = sem("s_pool")
    s_x = sem("s_x")
    s_wl = [sem(f"s_wl{i}") for i in range(8)]
    s_cst = {k: sem(f"s_c_{k}") for k in wscr}
    s_cc = sem("s_cc")
    s_ccl = sem("s_ccl")
    s_ccin = sem("s_ccin")
    s_out = sem("s_out")
    s_wb = [sem("s_wb0"), sem("s_wb1")]
    s_rwA = sem("s_rwA")
    s_rwB = sem("s_rwB")

    sb = lambda n, shp, dt: st.enter_context(nc.sbuf_tensor(n, shp, dt))
    res = sb("res", [128, IC * TPC], F32)
    rows1 = sb("rows1", [128, 1024], F32)
    rows2 = sb("rows2", [128, 512], F32)
    bfA = sb("bfA", [128, 1024], BF16)
    bfB = sb("bfB", [128, 1024], BF16)
    stg1 = sb("stg1", [128, 1024], F32)
    stg2 = sb("stg2", [128, 1024], F32)
    lnstg1 = sb("lnstg1", [128, 512], F32)
    lnstg2 = sb("lnstg2", [128, 512], F32)
    scl = sb("scl_sb", [128, 136], F32)
    eye_bf = sb("eye_sb", [128, 128], BF16)
    triu_f = sb("triu_sb", [128, 128], BF16)
    ones_bf = sb("ones_sb", [128, 1], BF16)
    onesr_f = sb("onesr_sb", [1, 128], F32)
    pmask = sb("pmask_sb", [128, 4], F32)
    den = sb("den_sb", [128, 8], F32)
    cst = sb("cst_sb", [128, 4], F32)
    nc.const_aps.register(F32, EPS_LN, cst[:, 0:1])
    nc.const_aps.register(F32, -0.5, cst[:, 1:2])
    nc.const_aps.register(F32, 0.0, cst[:, 2:3])
    nc.const_aps.register(F32, 1.0, cst[:, 3:4])

    s1c = lambda og: scl[:, og:og + 1]
    s2c = lambda og: scl[:, 32 + og:33 + og]
    sh1c = lambda og: scl[:, 64 + og:65 + og]
    sh2c = lambda og: scl[:, 96 + og:97 + og]
    s3c = lambda oc: scl[:, 128 + oc:129 + oc]

    ctxAttn = ExitStack()
    vhat = ctxAttn.enter_context(nc.sbuf_tensor("vhat", [128, NCH * (H * 65)], BF16))
    wsb = ctxAttn.enter_context(nc.sbuf_tensor("wsb", [128, 2 * 8192], BF16))
    ktmc = ctxAttn.enter_context(nc.sbuf_tensor("ktmc", [128, 2 * 1024], BF16))
    ST_A = ctxAttn.enter_context(nc.psum_tensor("ST_A", [128, 512], F32))
    ST_B = ctxAttn.enter_context(nc.psum_tensor("ST_B", [128, 512], F32))

    P = Prog()

    G = {"elu": None, "rows": [], "lnstg": None, "cst": None,
         "elu_i": 0, "elu_s": [None, None]}

    def t_mm(out, lhsT, rhs, deps, start=True, stop=True, tp=None):
        return P.add("tensor",
                     lambda e: e.matmul(out, lhsT, rhs, start=start, stop=stop,
                                        tile_position=tp),
                     deps, inc=s_pe)

    def t_tr(out, in_, deps):
        return P.add("tensor", lambda e: e.transpose(out, in_, eye_bf[:, :]),
                     deps, inc=s_pe)

    def v_op(fn, deps):
        return P.add("vector", fn, deps, inc=s_dve)

    def a_op(fn, deps):
        return P.add("scalar", fn, deps, inc=s_act)

    def g_op(fn, deps):
        return P.add("gpsimd", fn, deps, inc=s_pool)

    def dma(eng, out, in_, deps, semobj):
        return P.add(eng, lambda e: e.dma_start(out=out, in_=in_), deps,
                     inc=semobj, inck=16)

    def elu1(ps, out, dep, W, extra=()):
        sl = G["elu_i"] % 2
        G["elu_i"] += 1
        o = sl * 512
        s1 = stg1[:, o:o + W]
        s2 = stg2[:, o:o + W]
        deps = [dep, G["elu_s"][sl]] + list(extra)
        a1 = a_op(lambda e: e.activation(s1, ps, AF.Relu), deps)
        v1 = v_op(lambda e: e.tensor_tensor(s2, ps, s1, ALU.subtract), [a1])
        a2 = a_op(lambda e: e.activation(s2, s2, AF.Exp), [v1])
        v2 = v_op(lambda e: e.tensor_tensor(out, s1, s2, ALU.add), [a2])
        G["elu_s"][sl] = v2
        return v2

    def ln_rows(W, sx, sq, bc0, bc1, dep_stats, save_g=None):
        mu = rows1[0:1, 0:W]
        tmp = rows1[0:1, 512:512 + W]
        var = rows2[0:1, 0:W]
        r_ = var
        mr = mu
        rw = list(G["rows"]) + ([G["lnstg"]] if G["lnstg"] else [])
        t1 = v_op(lambda e: e.tensor_scalar(mu, sx, 1.0 / D, None, ALU.mult), [dep_stats] + rw)
        t2 = v_op(lambda e: e.tensor_scalar(var, sq, 1.0 / D, None, ALU.mult), [dep_stats] + rw)
        t3 = v_op(lambda e: e.tensor_tensor(tmp, mu, mu, ALU.mult), [t1])
        t4 = v_op(lambda e: e.tensor_tensor(var, var, tmp, ALU.subtract), [t2, t3])
        a1 = a_op(lambda e: e.activation(r_, var, AF.Ln, bias=EPS_LN), [t4, G["cst"]] + rw)
        a2 = a_op(lambda e: e.activation(r_, r_, AF.Exp, scale=-0.5), [a1])
        bc_r = t_mm(bc0, onesr_f[:, :], r_, [a2] + rw)
        t5 = v_op(lambda e: e.tensor_tensor(mr, mu, r_, ALU.mult), [a2])
        bc_m = t_mm(bc1, onesr_f[:, :], mr, [t5])
        G["rows"] = [bc_r, bc_m, t5]
        if save_g is not None:
            t0s = save_g * 512
            sr = dma("scalar", rows_d[0:1, t0s:t0s + 512], r_, [a2], s_rwA)
            sm = dma("scalar", rows_d[1:2, t0s:t0s + 512], mr, [t5], s_rwA)
            G["rows"] += [sr, sm]
        return bc_r, bc_m

    def ln_stats(W, src_f32, sx, sq, extra_dep, sq_dep=None):
        m1 = m2 = None
        prev = []
        for ic in range(IC):
            o = (ic % 2) * 512
            b_x = bfA[:, o:o + W]
            b_q = bfB[:, o:o + W]
            war = [prev[-2]] if len(prev) >= 2 else []
            dep = ([extra_dep] if extra_dep else []) + war
            tc = v_op(lambda e, o=b_x, i=src_f32(ic): e.tensor_copy(o, i), dep)
            tq = v_op(lambda e, o=b_q, i=b_x: e.tensor_tensor(o, i, i, ALU.mult), [tc])
            m1 = t_mm(sx, ones_bf[:, :], b_x, [tc], start=(ic == 0), stop=(ic == IC - 1))
            m2 = t_mm(sq, ones_bf[:, :], b_q,
                      [tq] + ([sq_dep] if (ic == 0 and sq_dep) else []),
                      start=(ic == 0), stop=(ic == IC - 1))
            prev.append(m2)
        return m2, m1

    def ln_apply(W, src_f32, out_bf, bc0, bc1, bc_toks, n=IC):
        toks = []
        for ic in range(n):
            dep = [bc_toks[0], bc_toks[1], G["lnstg"]] + ([toks[-1]] if toks else [])
            p1 = v_op(lambda e, i=src_f32(ic): e.tensor_tensor(lnstg1[:, 0:W], i, bc0, ALU.mult), dep)
            p2 = v_op(lambda e, o=out_bf(ic): e.tensor_tensor(o, lnstg1[:, 0:W], bc1, ALU.subtract), [p1])
            toks.append(p2)
        G["lnstg"] = toks[-1]
        return toks

    ctxA = ExitStack()
    sbA = lambda n, shp, dt: ctxA.enter_context(nc.sbuf_tensor(n, shp, dt))
    psA = lambda n, shp, dt=F32: ctxA.enter_context(nc.psum_tensor(n, shp, dt))
    wstg = sbA("wstg", [128, 2048], F32)
    xlnA = sbA("xlnA", [128, IC * 512], BF16)
    sst = sbA("sst", [128, 520], F32)
    PR0 = psA("PR0", [128, 512])
    PR1 = psA("PR1", [128, 512])
    BC0 = psA("BC0", [128, 512])
    BC1 = psA("BC1", [128, 512])
    STT = psA("STT", [1, 1024])

    c1 = v_op(lambda e: e.memset(cst[:, 0:1], EPS_LN), [])
    c2 = v_op(lambda e: e.memset(cst[:, 1:2], -0.5), [c1])
    c3 = v_op(lambda e: e.memset(cst[:, 2:3], 0.0), [c2])
    G["cst"] = v_op(lambda e: e.memset(cst[:, 3:4], 1.0), [c3])
    for ic in range(IC):
        dma("sync", res[:, ic * TPC:(ic + 1) * TPC], x_d[ic * 128:(ic + 1) * 128, :], [], s_x)
    dma("sync", scl[:, :], scl_d, [], s_x)
    dma("sync", eye_bf[:, :], eye_d, [], s_x)
    dma("sync", triu_f[:, :], triu_d, [], s_x)
    dma("sync", ones_bf[:, :], onesc_d, [], s_x)
    dma("sync", onesr_f[:, :], onesr_d, [], s_x)
    dma("sync", pmask[:, :], pmask_d, [], s_x)
    tok_x = (s_x, s_x.n)

    def load_w_cast(w_dram, slot_base, prev):
        casts = list(prev)
        for ic in range(IC):
            half = (ic % 2) * 1024
            dep = [casts[-2]] if len(casts) >= 2 else []
            ld = dma("sync", wstg[:, half:half + 1024],
                     w_dram[ic * 128:(ic + 1) * 128, :], dep, s_wb[ic % 2])
            c = v_op(lambda e, o=wsb[:, slot_base + ic * 1024: slot_base + (ic + 1) * 1024],
                     i=wstg[:, half:half + 1024]: e.tensor_copy(o, i), [ld])
            casts.append(c)
        return casts[-1], casts[-2:]

    tok_wk, chain = load_w_cast(wk_d, 0, [])
    tok_wv, chain = load_w_cast(wv_d, 8192, chain)

    cast_toks = {}
    first_cast = True
    for name in ["wo", "w1", "w2", "h1", "h2", "w3"]:
        n = {"wo": 8, "w1": 32, "w2": 32, "h1": 32, "h2": 32, "w3": 8}[name]
        for og in range(n):
            deps = [tok_x, tok_wk, tok_wv] if first_cast else []
            first_cast = False
            dma("gpsimd", wscr[name][og], wsrc[name][og], deps, s_cst[name])
        cast_toks[name] = (s_cst[name], s_cst[name].n)

    vhat_ones = vhat.ap().rearrange("p (c h o) -> p (c h) o", c=NCH, h=H)[:, :, 64:65]
    tok_vones = v_op(lambda e: e.memset(vhat_ones, 1.0), [])

    pr_war = {0: None, 1: None}
    state_last = None
    k_ev_last = None
    prev_group_tail = tok_x
    tok_sst0 = v_op(lambda e: e.memset(sst[:, :], 0.0), [])
    sst_add = tok_sst0
    for g in range(4):
        t0 = g * 512
        src = lambda ic, t0=t0: res[:, ic * TPC + t0: ic * TPC + t0 + 512]
        xo = lambda ic: xlnA[:, ic * 512:(ic + 1) * 512]
        stats_dep = prev_group_tail
        m_last, _ = ln_stats(512, src, STT[0:1, 0:512], STT[0:1, 512:1024], stats_dep)
        bc = ln_rows(512, STT[0:1, 0:512], STT[0:1, 512:1024], BC0[:, :], BC1[:, :], m_last,
                     save_g=g)
        ap_toks = ln_apply(512, src, xo, BC0[:, :], BC1[:, :], bc)
        for cc4 in range(4):
            c = g * 4 + cc4
            v_ev = {}
            for eh in range(2):
                pr = PR0 if eh == 0 else PR1
                mmt = None
                for ic in range(IC):
                    lhs = xlnA[:, ic * 512 + cc4 * 128: ic * 512 + cc4 * 128 + 128]
                    rhs = wsb[:, 8192 + ic * 1024 + eh * 512: 8192 + ic * 1024 + eh * 512 + 512]
                    deps = [ap_toks[ic], tok_wv] if ic == 0 else []
                    if ic == 0:
                        deps.append(pr_war[eh])
                    else:
                        deps = [ap_toks[ic]]
                    mmt = t_mm(pr[:, :], lhs, rhs, deps, start=(ic == 0), stop=(ic == IC - 1))
                vv = vhat.ap().rearrange("p (c h o) -> p c h o", c=NCH, h=H)[:, c, eh * 8:(eh + 1) * 8, 0:64]
                prv = pr.ap().rearrange("p (h o) -> p h o", h=8)
                ev = v_op(lambda e, o=vv, i=prv: e.tensor_copy(o, i), [mmt, tok_vones])
                pr_war[eh] = ev
                v_ev[eh] = ev
            kslot = ktmc[:, (c % 2) * 1024:(c % 2) * 1024 + 1024]
            k_ev = {}
            for eh in range(2):
                pr = PR0 if eh == 0 else PR1
                mmt = None
                for ic in range(IC):
                    lhs = xlnA[:, ic * 512 + cc4 * 128: ic * 512 + cc4 * 128 + 128]
                    rhs = wsb[:, ic * 1024 + eh * 512: ic * 1024 + eh * 512 + 512]
                    deps = [ap_toks[ic], tok_wk, pr_war[eh]] if ic == 0 else [ap_toks[ic]]
                    mmt = t_mm(pr[:, :], lhs, rhs, deps, start=(ic == 0), stop=(ic == IC - 1))
                ev = elu1(pr[:, :], kslot[:, eh * 512:eh * 512 + 512], mmt, 512,
                          extra=[state_last])
                pr_war[eh] = ev
                k_ev[eh] = ev
            k_ev_last = k_ev[1]
            for h in range(H):
                lhs = kslot[:, h * 64:(h + 1) * 64]
                rhs = vhat[:, c * (H * 65) + h * 65: c * (H * 65) + (h + 1) * 65]
                stp = ST_A if (h // 2) % 2 == 0 else ST_B
                pbase = 64 * (h % 2)
                slot = h // 4
                out = stp[pbase:pbase + 64, slot * 65:(slot + 1) * 65]
                tp = (0, pbase) if pbase else None
                state_last = t_mm(out, lhs, rhs,
                                  [k_ev[h // 8], v_ev[h // 8], sst_add],
                                  start=True, stop=True, tp=tp)
            vA = bass.AP(sst, 0, [[520, 128], [130, 4], [1, 65]])
            vB = bass.AP(sst, 65, [[520, 128], [130, 4], [1, 65]])
            sA = bass.AP(ST_A, 0, [[512, 128], [65, 4], [1, 65]])
            sB = bass.AP(ST_B, 0, [[512, 128], [65, 4], [1, 65]])
            a1_ = v_op(lambda e: e.tensor_tensor(vA, vA, sA, ALU.add), [state_last])
            sst_add = v_op(lambda e: e.tensor_tensor(vB, vB, sB, ALU.add), [state_last])
        prev_group_tail = state_last

    d1 = dma("sync", cc_in, sst[:, :], [sst_add], s_ccin)
    cc_tok = P.add("gpsimd",
                   lambda e: e.collective_compute(
                       "AllGather", ALU.bypass,
                       replica_groups=[[0, 1, 2, 3], [4, 5, 6, 7]],
                       ins=[cc_in], outs=[cc_out]),
                   [d1], inc=s_cc)

    if stage < 2:
        P.add("gpsimd", lambda e: e.memset(den[:, 0:1], 0.0), [(s_cc, s_cc.n)])
    with nc.Block("pA", no_gpsimd_drain=True) as blkA:
        P.flush(blkA)
    ctxA.close()
    if stage < 2:
        ctxAttn.close()
        st.close()
        return nc

    ctxB = ExitStack()
    sbB = lambda n, shp, dt: ctxB.enter_context(nc.sbuf_tensor(n, shp, dt))
    psB = lambda n, shp, dt=F32: ctxB.enter_context(nc.psum_tensor(n, shp, dt))
    xlnB = sbB("xlnB", [128, 2 * IC * 256], BF16)
    qst = sbB("qst", [128, 2 * 8 * 256], BF16)
    kfm = sbB("kfm", [128, 1024], BF16)
    atm = sbB("atm", [128, 2 * 512], BF16)
    ytm = sbB("ytm", [128, 1024], BF16)
    yfm = sbB("yfm", [128, 2 * 8 * 256], BF16)
    wos = sbB("wos", [128, 2 * 1024], BF16)
    aof = sbB("aof", [128, 8 * 256], F32)
    aob = sbB("aob", [128, 8 * 256], BF16)
    sinit = sbB("sinit", [128, 520], F32)
    ccg = sbB("ccg", [128, 520], F32)
    ssb = sbB("ssb", [128, 520], BF16)

    PS_tr = psB("PS_tr", [128, 1024], BF16)
    PS_at = psB("PS_at", [128, 512])
    PS_y = psB("PS_y", [128, 260])
    PR0b = psB("PR0b", [128, 512])
    PR1b = psB("PR1b", [128, 512])
    PSm = psB("PSm", [128, 512])

    wq_casts = list(chain)
    for ic in range(IC):
        half = (ic % 2) * 1024
        dep = [wq_casts[-2]] if len(wq_casts) >= 2 else []
        ld = dma("sync", aof[:, half:half + 1024], wq_d[ic * 128:(ic + 1) * 128, :], dep, s_wb[ic % 2])
        c = v_op(lambda e, o=wsb[:, 8192 + ic * 1024: 8192 + (ic + 1) * 1024],
                 i=aof[:, half:half + 1024]: e.tensor_copy(o, i), [ld])
        wq_casts.append(c)
    tok_wq = wq_casts[-1]

    t_ms = v_op(lambda e: e.memset(sinit[:, :], 0.0), [])
    acc = t_ms
    for r in range(4):
        ld = dma("sync", ccg[:, :], cc_out[r * 128:(r + 1) * 128, :], [(s_cc, s_cc.n), acc], s_ccl)
        acc = v_op(lambda e, m=pmask[:, r:r + 1]: e.scalar_tensor_tensor(
            sinit[:, :], ccg[:, :], m, sinit[:, :], ALU.mult, ALU.add), [ld, acc])
    tok_sinit = acc

    pr_warB = {0: tok_wq, 1: tok_wq}
    state_lastB = None
    state_addB = None
    ssb_tok = None
    pa_tail = sst_add
    PB = {"psat": None}
    rowld = {}

    def emit_rowload(g2l, deps):
        if g2l >= 8:
            return
        t0l = g2l * 256
        l1 = dma("sync", rows2[0:1, 256:512], rows_d[0:1, t0l:t0l + 256],
                 deps + [(s_rwA, 16 * 8)], s_rwB)
        l2 = dma("sync", rows1[0:1, 768:1024], rows_d[1:2, t0l:t0l + 256],
                 deps, s_rwB)
        rowld[g2l] = (l1, l2)
    ytr_war = None
    kf_war = None
    y_div_last = None
    ln2_last = None
    wo_war = {0: None, 1: None}
    yfm_war = {0: None, 1: None}
    qst_war = {0: None, 1: None}
    xln_war = {0: None, 1: None}
    aof_war = tok_wq
    k_ev_lastB = None

    for g2 in range(8):
        t0 = g2 * 256
        slot = g2 % 2
        src = lambda ic, t0=t0: res[:, ic * TPC + t0: ic * TPC + t0 + 256]
        xo = lambda ic, slot=slot: xlnB[:, slot * IC * 256 + ic * 256: slot * IC * 256 + (ic + 1) * 256]
        if g2 == 0:
            emit_rowload(0, [])
        bc_deps = [rowld[g2][0], rowld[g2][1], PB["psat"]]
        if g2 == 0:
            bc_deps.append(pa_tail)
        bc_r = t_mm(PS_at[:, 0:256], onesr_f[:, :], rows2[0:1, 256:512], bc_deps)
        bc_m = t_mm(PS_at[:, 256:512], onesr_f[:, :], rows1[0:1, 768:1024], [])
        bc = (bc_r, bc_m)
        ap_toks = ln_apply(256, src, xo, PS_at[:, 0:256], PS_at[:, 256:512], bc)
        xln_done = ap_toks
        emit_rowload(g2 + 1, [bc_r, bc_m])
        q_toks = [None] * 8
        for p in range(4):
            pr = PR0b if p % 2 == 0 else PR1b
            mmt = None
            for half in range(2):
                dc = 2 * p + half
                for ic in range(IC):
                    lhsT = wsb[:, 8192 + ic * 1024 + dc * 128: 8192 + ic * 1024 + dc * 128 + 128]
                    rhs = xlnB[:, slot * IC * 256 + ic * 256: slot * IC * 256 + (ic + 1) * 256]
                    deps = [xln_done[ic]]
                    if ic == 0 and half == 0:
                        deps += [tok_wq, pr_warB[p % 2]]
                    mmt = t_mm(pr[:, half * 256:half * 256 + 256], lhsT, rhs, deps,
                               start=(ic == 0), stop=(ic == IC - 1))
            qo = qst[:, slot * 2048 + 2 * p * 256: slot * 2048 + (2 * p + 2) * 256]
            ev = elu1(pr[:, 0:512], qo, mmt, 512, extra=[qst_war[slot], state_lastB])
            pr_warB[p % 2] = ev
            q_toks[2 * p] = q_toks[2 * p + 1] = ev
        qst_war[slot] = q_toks[-1]
        for ci in (range(2) if pb_parts >= 2 else []):
            c = 2 * g2 + ci
            kslot = ktmc[:, (c % 2) * 1024:(c % 2) * 1024 + 1024]
            k_ev = {}
            for eh in range(2):
                pr = PR0b if eh == 0 else PR1b
                mmt = None
                for ic in range(IC):
                    lhsT = xlnB[:, slot * IC * 256 + ic * 256 + ci * 128: slot * IC * 256 + ic * 256 + ci * 128 + 128]
                    rhs = wsb[:, ic * 1024 + eh * 512: ic * 1024 + eh * 512 + 512]
                    deps = [xln_done[ic]]
                    if ic == 0:
                        deps += [tok_wk, pr_warB[eh]]
                    mmt = t_mm(pr[:, :], lhsT, rhs, deps, start=(ic == 0), stop=(ic == IC - 1))
                ev = elu1(pr[:, :], kslot[:, eh * 512:eh * 512 + 512], mmt, 512,
                          extra=[state_lastB])
                pr_warB[eh] = ev
                k_ev[eh] = ev
            k_ev_lastB = k_ev[1]
            ktr = []
            for j in range(8):
                deps = [k_ev[j // 4]]
                if ytr_war is not None and j == 0:
                    deps.append(ytr_war)
                ktr.append(t_tr(PS_tr[:, j * 128:(j + 1) * 128],
                                kslot[:, j * 128:(j + 1) * 128], deps))
            kf = kfm[:, 0:1024]
            kf_tok = v_op(lambda e, o=kf: e.tensor_copy(o, PS_tr[:, :]),
                          ktr + [y_div_last])
            if pc < 2:
                ytr_war = kf_tok
                continue
            sdep = [tok_sinit if c == 0 else state_addB]
            ssb_tok = v_op(lambda e: e.tensor_copy(ssb[:, :], sinit[:, :]), sdep)
            if pc < 3:
                ytr_war = ssb_tok
                continue
            waves = [[0, 2, 4, 6], [8, 10, 12, 14], [1, 3, 5, 7], [9, 11, 13, 15]]
            wave_mk = {}
            atm_war = {0: None, 1: None}
            at_war = None

            def emit_at(w):
                nonlocal at_war
                heads = waves[w]
                pb = 64 * (heads[0] % 2)
                at_toks = []
                for j, h in enumerate(heads):
                    lhsT = kf[pb:pb + 64, (h // 2) * 128:(h // 2) * 128 + 128]
                    rhs = qst[pb:pb + 64,
                              slot * 2048 + (h // 2) * 256 + ci * 128: slot * 2048 + (h // 2) * 256 + ci * 128 + 128]
                    deps = [kf_tok, q_toks[h // 2]]
                    if j == 0:
                        deps.append(at_war)
                        deps.append(xln_done[-1])
                    at_toks.append(t_mm(PS_at[:, j * 128:(j + 1) * 128], lhsT, rhs, deps))
                if pc == 36 or (pc == 38 and w == 1):
                    at_war = at_toks[-1]
                    wave_mk[w] = at_toks
                    return
                am = atm[:, (w % 2) * 512:(w % 2) * 512 + 512]
                mks = []
                for j, h in enumerate(heads):
                    mks.append(v_op(lambda e, j=j, am=am: e.tensor_tensor(
                        am[:, j * 128:(j + 1) * 128], PS_at[:, j * 128:(j + 1) * 128],
                        triu_f[:, :], ALU.mult), [at_toks[-1], atm_war[w % 2]]))
                wave_mk[w] = mks
                at_war = mks[-1]
                PB["psat"] = mks[-1]

            def emit_y(w):
                nonlocal y_div_last
                heads = waves[w]
                pb = 64 * (heads[0] % 2)
                am = atm[:, (w % 2) * 512:(w % 2) * 512 + 512]
                mks = wave_mk[w]
                ymm = []
                for j, h in enumerate(heads):
                    m1 = t_mm(PS_y[:, j * 65:(j + 1) * 65],
                              am[:, j * 128:(j + 1) * 128],
                              vhat[:, c * (H * 65) + h * 65: c * (H * 65) + (h + 1) * 65],
                              [mks[j]] + ([y_div_last] if j == 0 else []),
                              start=True, stop=False)
                    m2 = t_mm(PS_y[:, j * 65:(j + 1) * 65],
                              qst[pb:pb + 64, slot * 2048 + (h // 2) * 256 + ci * 128: slot * 2048 + (h // 2) * 256 + ci * 128 + 128],
                              ssb[pb:pb + 64, 65 * (h // 2):65 * (h // 2) + 65],
                              [ssb_tok], start=False, stop=True)
                    ymm.append(m2)
                atm_war[w % 2] = ymm[-1]
                dn = bass.AP(PS_y, 64, [[260, 128], [65, 4]])
                d0 = v_op(lambda e: e.tensor_scalar(den[:, 0:4], dn, EPS_ATT, None, ALU.add), ymm)
                d1_ = v_op(lambda e: e.reciprocal(den[:, 4:8], den[:, 0:4]), [d0])
                for j, h in enumerate(heads):
                    y_div_last = v_op(lambda e, j=j, h=h: e.tensor_scalar(
                        ytm[:, h * 64:(h + 1) * 64],
                        PS_y[:, j * 65:j * 65 + 64], den[:, 4 + j:5 + j], None, ALU.mult), [d1_])

            def pe_fence():
                v = s_pe.n
                P.add("tensor", lambda e, v=v: e.wait_ge(s_pe.h, v), [])

            def dbg_dump(src_ap, off, deps, eng="sync"):
                n = src_ap.free_size()
                sem_ = s_out if eng == "sync" else s_cst["wo"]
                return dma(eng, bass.AP(dbg_d.tensor, off, [[3072, 128], [1, n]]),
                           src_ap, deps, sem_)

            emit_at(0)
            if pc not in (36, 37):
                emit_at(1)
            if pc >= 5 and pc not in (36, 37, 38):
                emit_y(0)
                emit_y(1)
            if dbg and c == 0:
                t1 = dbg_dump(atm[:, 0:512], 0, [wave_mk[0][-1], atm_war[0]])
                t2 = dbg_dump(atm[:, 512:1024], 512, [wave_mk[1][-1], atm_war[1]])
                atm_war[0] = t1
                atm_war[1] = t2
            if pc not in (35, 36, 37, 38):
                pe_fence()
                emit_at(2)
                emit_at(3)
                pe_fence()
                if pc >= 5:
                    emit_y(2)
                    emit_y(3)
            if pc < 5:
                y_div_last = at_war
            if pc < 6:
                state_addB = y_div_last
                ytr_war = y_div_last
                continue
            if dbg and c == 0:
                dbg_ytm = dbg_dump(ytm[:, 0:1024], 1024, [y_div_last])
                dbg_dump(den[:, 0:8], 2048, [y_div_last], eng="gpsimd")
                dbg_dump(ssb[:, 0:520], 2100, [ssb_tok])
            for h in range(H):
                lhsT = kslot[:, h * 64:(h + 1) * 64]
                rhs = vhat[:, c * (H * 65) + h * 65: c * (H * 65) + (h + 1) * 65]
                stp = ST_A if (h // 2) % 2 == 0 else ST_B
                pbase = 64 * (h % 2)
                slot_h = h // 4
                out = stp[pbase:pbase + 64, slot_h * 65:(slot_h + 1) * 65]
                tp = (0, pbase) if pbase else None
                state_lastB = t_mm(out, lhsT, rhs,
                                   [ssb_tok, k_ev[h // 8], state_addB],
                                   start=True, stop=True, tp=tp)
            if c < NCH - 1:
                vA = bass.AP(sinit, 0, [[520, 128], [130, 4], [1, 65]])
                vB = bass.AP(sinit, 65, [[520, 128], [130, 4], [1, 65]])
                sA = bass.AP(ST_A, 0, [[512, 128], [65, 4], [1, 65]])
                sB = bass.AP(ST_B, 0, [[512, 128], [65, 4], [1, 65]])
                v_op(lambda e, vA=vA, sA=sA: e.tensor_tensor(vA, vA, sA, ALU.add), [state_lastB])
                state_addB = v_op(lambda e, vB=vB, sB=sB: e.tensor_tensor(vB, vB, sB, ALU.add), [state_lastB])
            ytr = []
            for j in range(8):
                deps = [y_div_last, kf_tok] if j == 0 else [y_div_last]
                ytr.append(t_tr(PS_tr[:, j * 128:(j + 1) * 128],
                                ytm[:, j * 128:(j + 1) * 128],
                                deps))
            yf_dst = bass.AP(yfm, slot * 2048 + ci * 128, [[4096, 128], [256, 8], [1, 128]])
            ptr_src = PS_tr.ap().rearrange("p (j t) -> p j t", j=8)
            ytr_war = v_op(lambda e, o=yf_dst, i=ptr_src: e.tensor_copy(o, i),
                           ytr + [yfm_war[slot]])
            yfm_done = ytr_war
        if pb_parts < 3:
            continue
        wo_evac = []
        wo_mm_last = None
        for oc in range(8):
            wslot = wos[:, (oc % 2) * 1024:(oc % 2) * 1024 + 1024]
            dep_w = [cast_toks["wo"]] + ([wo_evac[-2]] if len(wo_evac) >= 2 else [wo_war[oc % 2]])
            ld = dma("sync", wslot, wscr["wo"][oc].rearrange("p a b -> p (a b)"), dep_w, s_wl[oc % 2])
            pr = PR0b if oc % 2 == 0 else PR1b
            mmt = None
            for ec in range(8):
                deps = [ld, yfm_done] if ec == 0 else []
                if ec == 0:
                    deps.append(pr_warB[oc % 2])
                mmt = t_mm(pr[:, 0:256], wslot[:, ec * 128:(ec + 1) * 128],
                           yfm[:, slot * 2048 + ec * 256: slot * 2048 + (ec + 1) * 256],
                           deps, start=(ec == 0), stop=(ec == 7))
            wo_mm_last = mmt
            a1 = a_op(lambda e, oc=oc, pr=pr: e.activation(
                aof[:, oc * 256:(oc + 1) * 256], pr[:, 0:256], AF.Copy),
                [mmt, aof_war])
            v1 = v_op(lambda e, oc=oc, pr=pr: e.tensor_copy(
                aob[:, oc * 256:(oc + 1) * 256], pr[:, 0:256]), [mmt, a1])
            pr_warB[oc % 2] = v1
            wo_evac.append(v1)
        yfm_war[slot] = wo_mm_last
        wo_war[0] = wo_evac[-2]
        wo_war[1] = wo_evac[-1]
        sx = PSm[0:1, 0:256]
        sq = PR0b[0:1, 256:512]
        m1 = m2 = None
        m2s = []
        for oc in range(8):
            o = (oc % 2) * 512
            war = [m2s[-2]] if len(m2s) >= 2 else []
            tq = v_op(lambda e, oc=oc, o=o: e.tensor_tensor(
                bfB[:, o:o + 256], aob[:, oc * 256:(oc + 1) * 256],
                aob[:, oc * 256:(oc + 1) * 256], ALU.mult), [wo_evac[oc]] + war)
            deps1 = [wo_evac[oc]] + ((G["rows"] + [G["lnstg"]]) if oc == 0 else [])
            m1 = t_mm(sx, ones_bf[:, :], aob[:, oc * 256:(oc + 1) * 256], deps1,
                      start=(oc == 0), stop=(oc == 7))
            m2 = t_mm(sq, ones_bf[:, :], bfB[:, o:o + 256],
                      [tq] + ([wo_evac[-2]] if oc == 0 else []),
                      start=(oc == 0), stop=(oc == 7))
            m2s.append(m2)
        ln2_m2_last = m2
        bc = ln_rows(256, sx, sq, PSm[:, 0:256], PSm[:, 256:512], m2)
        for oc in range(8):
            dep = [bc[0], bc[1], G["lnstg"]] + ([ln2_last] if ln2_last else [])
            p1 = v_op(lambda e, oc=oc: e.tensor_tensor(
                lnstg1[:, 0:256], aof[:, oc * 256:(oc + 1) * 256], PSm[:, 0:256], ALU.mult), dep)
            p2 = v_op(lambda e: e.tensor_tensor(lnstg2[:, 0:256], lnstg1[:, 0:256],
                                                PSm[:, 256:512], ALU.subtract), [p1])
            p3 = g_op(lambda e, oc=oc, t0=t0: e.tensor_tensor(
                res[:, oc * TPC + t0: oc * TPC + t0 + 256],
                res[:, oc * TPC + t0: oc * TPC + t0 + 256], lnstg2[:, 0:256], ALU.add), [p2])
            G["lnstg"] = p2
            ln2_last = p3
        aof_war = ln2_last

    if stage < 3:
        tail = ln2_last if pb_parts >= 3 else (
            (state_addB if state_addB else ytr_war) if pb_parts >= 2 else qst_war[1])
        P.add("gpsimd", lambda e: e.memset(den[:, 0:1], 0.0),
              [tail, (s_pe, s_pe.n), (s_dve, s_dve.n), (s_act, s_act.n)])
    with nc.Block("pB", no_gpsimd_drain=True) as blkB:
        P.flush(blkB)
    ctxB.close()
    ctxAttn.close()
    if stage < 3:
        st.close()
        return nc

    ctxF = ExitStack()
    sbF = lambda n, shp, dt: ctxF.enter_context(nc.sbuf_tensor(n, shp, dt))
    psF = lambda n, shp, dt=F32: ctxF.enter_context(nc.psum_tensor(n, shp, dt))
    hbuf = sbF("hbuf", [128, HC * 512], BF16)
    gbuf = sbF("gbuf", [128, HC * 512], BF16)
    xln3 = sbF("xln3", [128, IC * 512], BF16)
    wbig = sbF("wbig", [128, 2 * 4096], BF16)
    wsml = sbF("wsml", [128, 4 * 1024], BF16)
    silu_s = sbF("silu_s", [128, 512], BF16)
    y3f = sbF("y3f", [128, 8 * 512], F32)

    PRf = [psF(f"PRf{i}", [128, 512]) for i in range(4)]
    PSmf = psF("PSmf", [1, 1024])
    BCf0 = psF("BCf0", [128, 512])
    BCf1 = psF("BCf1", [128, 512])

    out_count = 0
    prev_tail = ln2_last
    wslot_war = {k: None for k in ["s0", "s1", "s2", "s3", "b0", "b1"]}
    for s in range(NSL):
        t0 = s * TS
        src = lambda ic, t0=t0: res[:, ic * TPC + t0: ic * TPC + t0 + TS]
        xo = lambda ic: xln3[:, ic * 512:(ic + 1) * 512]
        m_last, _ = ln_stats(512, src, PSmf[0:1, 0:512], PSmf[0:1, 512:1024], prev_tail)
        bc = ln_rows(512, PSmf[0:1, 0:512], PSmf[0:1, 512:1024], PRf[2][:, :], PRf[3][:, :], m_last)
        xln_done = ln_apply(512, src, xo, PRf[2][:, :], PRf[3][:, :], bc)
        h_toks = []
        for og in range(HC):
            sl = og % 2
            w1b = wsml[:, sl * 1024: sl * 1024 + 1024]
            w2b = wsml[:, 2048 + sl * 1024: 2048 + sl * 1024 + 1024]
            pr_war_ = [h_toks[-2]] if len(h_toks) >= 2 else [xln_done[-1]]
            ld_eng = "gpsimd" if (s == 0 and og < 2) else "sync"
            l1 = dma(ld_eng, w1b, wscr["w1"][og].rearrange("p a b -> p (a b)"),
                     [cast_toks["w1"], wslot_war["s" + str(sl)]], s_wl[2 + sl])
            l2 = dma(ld_eng, w2b, wscr["w2"][og].rearrange("p a b -> p (a b)"),
                     [cast_toks["w2"], wslot_war["s" + str(2 + sl)]], s_wl[4 + sl])
            pra = PRf[sl * 2]
            prb = PRf[sl * 2 + 1]
            ma = mb = None
            for ic in range(IC):
                deps = ([l1, xln_done[ic]] + pr_war_) if ic == 0 else [xln_done[ic]]
                ma = t_mm(pra[:, :], w1b[:, ic * 128:(ic + 1) * 128], xo(ic),
                          deps, start=(ic == 0), stop=(ic == IC - 1))
            wslot_war["s" + str(sl)] = ma
            sa = a_op(lambda e, pra=pra, og=og: e.activation(
                silu_s[:, :], pra[:, :], AF.Silu, scale=s1c(og)),
                [ma, G["elu"]] + ([h_toks[-1]] if h_toks else []))
            G["elu"] = sa
            for ic in range(IC):
                deps = ([l2, xln_done[ic]] + pr_war_) if ic == 0 else [xln_done[ic]]
                mb = t_mm(prb[:, :], w2b[:, ic * 128:(ic + 1) * 128], xo(ic),
                          deps, start=(ic == 0), stop=(ic == IC - 1))
            wslot_war["s" + str(2 + sl)] = mb
            hv = v_op(lambda e, prb=prb, og=og: e.scalar_tensor_tensor(
                hbuf[:, og * 512:(og + 1) * 512], prb[:, :], s2c(og), silu_s[:, :],
                ALU.mult, ALU.mult), [mb, sa])
            h_toks.append(hv)
        g_toks = []
        for og in range(HC):
            sl = og % 2
            wb = wbig[:, sl * 4096: sl * 4096 + 4096]
            ld_eng = "gpsimd" if (s == 0 and og < 2) else "sync"
            ld = dma(ld_eng, wb, wscr["h1"][og].rearrange("p a b -> p (a b)"),
                     [cast_toks["h1"], wslot_war["b" + str(sl)]], s_wl[6 + sl])
            pr = PRf[og % 3]
            mm = None
            for icp in range(HC):
                deps = [ld, h_toks[-1]] + ([g_toks[-3]] if len(g_toks) >= 3 else [])
                mm = t_mm(pr[:, :], wb[:, icp * 128:(icp + 1) * 128],
                          hbuf[:, icp * 512:(icp + 1) * 512],
                          deps if icp == 0 else [],
                          start=(icp == 0), stop=(icp == HC - 1))
            wslot_war["b" + str(sl)] = mm
            gv = a_op(lambda e, pr=pr, og=og: e.activation(
                gbuf[:, og * 512:(og + 1) * 512], pr[:, :], AF.Gelu, scale=sh1c(og)), [mm])
            g_toks.append(gv)
        u_toks = []
        for og in range(HC):
            sl = og % 2
            wb = wbig[:, sl * 4096: sl * 4096 + 4096]
            ld = dma("sync", wb, wscr["h2"][og].rearrange("p a b -> p (a b)"),
                     [cast_toks["h2"], wslot_war["b" + str(sl)]], s_wl[6 + sl])
            pr = PRf[og % 3]
            mm = None
            for icp in range(HC):
                deps = [ld, g_toks[-1]] + ([u_toks[-3]] if len(u_toks) >= 3 else [])
                mm = t_mm(pr[:, :], wb[:, icp * 128:(icp + 1) * 128],
                          gbuf[:, icp * 512:(icp + 1) * 512],
                          deps if icp == 0 else [],
                          start=(icp == 0), stop=(icp == HC - 1))
            wslot_war["b" + str(sl)] = mm
            uv = a_op(lambda e, pr=pr, og=og: e.activation(
                hbuf[:, og * 512:(og + 1) * 512], pr[:, :], AF.Gelu, scale=sh2c(og)), [mm])
            u_toks.append(uv)
        y3_toks = []
        stat_toks = []
        sx = PSmf[0:1, 0:512]
        sq = PSmf[0:1, 512:1024]
        m1 = m2 = None
        for oc in range(8):
            sl = oc % 2
            wb = wbig[:, sl * 4096: sl * 4096 + 4096]
            ld = dma("sync", wb, wscr["w3"][oc].rearrange("p a b -> p (a b)"),
                     [cast_toks["w3"], wslot_war["b" + str(sl)]], s_wl[6 + sl])
            pr = PRf[oc % 3]
            mm = None
            for icp in range(HC):
                deps = [ld, u_toks[-1]] + ([y3_toks[-3]] if len(y3_toks) >= 3 else [])
                mm = t_mm(pr[:, :], wb[:, icp * 128:(icp + 1) * 128],
                          hbuf[:, icp * 512:(icp + 1) * 512],
                          deps if icp == 0 else [],
                          start=(icp == 0), stop=(icp == HC - 1))
            wslot_war["b" + str(sl)] = mm
            af = a_op(lambda e, pr=pr, oc=oc: e.activation(
                y3f[:, oc * 512:(oc + 1) * 512], pr[:, :], AF.Copy, scale=s3c(oc)), [mm])
            o = sl * 512
            slot_war = [stat_toks[-2]] if len(stat_toks) >= 2 else []
            extra0 = (G["rows"] + [G["lnstg"]]) if oc == 0 else []
            vb = v_op(lambda e, pr=pr, oc=oc, o=o: e.tensor_scalar(
                bfA[:, o:o + 512], pr[:, :], s3c(oc), None, ALU.mult),
                [mm, af] + slot_war)
            tq = v_op(lambda e, o=o: e.tensor_tensor(
                bfB[:, o:o + 512], bfA[:, o:o + 512], bfA[:, o:o + 512], ALU.mult), [vb])
            m1 = t_mm(sx, ones_bf[:, :], bfA[:, o:o + 512], [vb] + extra0,
                      start=(oc == 0), stop=(oc == 7))
            m2 = t_mm(sq, ones_bf[:, :], bfB[:, o:o + 512], [tq],
                      start=(oc == 0), stop=(oc == 7))
            stat_toks.append(m2)
            y3_toks.append(vb)
        bc = ln_rows(512, sx, sq, BCf0[:, :], BCf1[:, :], m2)
        st_last = None
        for oc in range(8):
            dep = [bc[0], bc[1], G["lnstg"]] + ([st_last] if st_last else [])
            p1 = v_op(lambda e, oc=oc: e.tensor_tensor(
                lnstg1[:, :], y3f[:, oc * 512:(oc + 1) * 512], BCf0[:, :], ALU.mult), dep)
            p2 = v_op(lambda e: e.tensor_tensor(lnstg2[:, :], lnstg1[:, :], BCf1[:, :], ALU.subtract), [p1])
            p3 = g_op(lambda e, oc=oc, t0=t0: e.tensor_tensor(
                res[:, oc * TPC + t0: oc * TPC + t0 + TS],
                res[:, oc * TPC + t0: oc * TPC + t0 + TS], lnstg2[:, :], ALU.add), [p2])
            G["lnstg"] = p2
            st_last = p3
            dma("gpsimd", y_d[oc * 128:(oc + 1) * 128, t0:t0 + TS],
                res[:, oc * TPC + t0: oc * TPC + t0 + TS], [p3], s_out)
            out_count += 1
        prev_tail = bc[1]

    P.add("gpsimd", lambda e: e.memset(den[:, 0:1], 0.0), [(s_out, 16 * out_count)])

    with nc.Block("ffn") as blkF:
        P.flush(blkF)
    ctxF.close()
    st.close()
    return nc


def kernel(L0_attn_x, L0_attn_Wq, L0_attn_Wk, L0_attn_Wv, L0_attn_Wo,
           L0_ffn_w1_w, L0_ffn_w1_s, L0_ffn_w2_w, L0_ffn_w2_s,
           L0_ffn_h1_w, L0_ffn_h1_s, L0_ffn_h2_w, L0_ffn_h2_s,
           L0_ffn_w3_w, L0_ffn_w3_s):
    x = np.asarray(L0_attn_x, dtype=np.float32)
    B, T, _ = x.shape
    c = np.ascontiguousarray

    def blockw(w, icn):
        ogn = w.shape[0] // 128
        return c(w.reshape(ogn, 128, icn, 128).transpose(0, 3, 2, 1))

    shared = {
        "wq": c(np.asarray(L0_attn_Wq, np.float32)),
        "wk": c(np.asarray(L0_attn_Wk, np.float32)),
        "wv": c(np.asarray(L0_attn_Wv, np.float32)),
        "wbo": blockw(np.asarray(L0_attn_Wo, np.float32).T, 8),
        "wb1": blockw(np.asarray(L0_ffn_w1_w, np.float32), 8),
        "wb2": blockw(np.asarray(L0_ffn_w2_w, np.float32), 8),
        "wbh1": blockw(np.asarray(L0_ffn_h1_w, np.float32), 32),
        "wbh2": blockw(np.asarray(L0_ffn_h2_w, np.float32), 32),
        "wbw3": blockw(np.asarray(L0_ffn_w3_w, np.float32), 32),
        "eye_bf": np.eye(128, dtype=ml_dtypes.bfloat16),
        "triu_f": c(np.triu(np.ones((128, 128))).astype(ml_dtypes.bfloat16)),
        "ones_bf": np.ones((128, 1), ml_dtypes.bfloat16),
        "onesr_f": np.ones((1, 128), np.float32),
    }
    scl = np.zeros((128, 136), np.float32)
    scl[:, 0:32] = np.asarray(L0_ffn_w1_s, np.float32).reshape(32, 128).T
    scl[:, 32:64] = np.asarray(L0_ffn_w2_s, np.float32).reshape(32, 128).T
    scl[:, 64:96] = np.asarray(L0_ffn_h1_s, np.float32).reshape(32, 128).T
    scl[:, 96:128] = np.asarray(L0_ffn_h2_s, np.float32).reshape(32, 128).T
    scl[:, 128:136] = np.asarray(L0_ffn_w3_s, np.float32).reshape(8, 128).T
    shared["scl"] = c(scl)

    in_maps = []
    for core in range(8):
        b, q = core // 4, core % 4
        m = dict(shared)
        m["x"] = c(x[b, q * TPC:(q + 1) * TPC, :].T)
        pm = np.zeros((128, 4), np.float32)
        pm[:, :q] = 1.0
        m["pmask"] = pm
        in_maps.append(m)

    nc = build_nc()
    res = run_bass_kernel_spmd(nc, in_maps, core_ids=list(range(8)))
    out = np.empty((B, T, D), np.float32)
    for core in range(8):
        b, q = core // 4, core % 4
        out[b, q * TPC:(q + 1) * TPC, :] = res.results[core]["y"].T
    return out




# version 14
# speedup vs baseline: 1.0008x; 1.0008x over previous
import numpy as np
import ml_dtypes
from contextlib import ExitStack

import concourse.bass as bass
import concourse.mybir as mybir
from concourse.bass_utils import run_bass_kernel_spmd

F32 = mybir.dt.float32
BF16 = mybir.dt.bfloat16
AF = mybir.ActivationFunctionType
ALU = mybir.AluOpType

D = 1024
TPC = 2048
H = 16
HD = 64
CH = 128
NCH = TPC // CH
IC = D // 128
DH = 4096
HC = DH // 128
NSL = 4
TS = TPC // NSL
EPS_ATT = 1e-6
EPS_LN = 1e-5


def soff(h):
    return 260 * ((h // 2) % 2) + 65 * (h // 4)


class Sem:
    def __init__(self, handle, name, by_order=False):
        self.h = handle
        self.name = name
        self.n = 0
        self.by_order = by_order


class Op:
    __slots__ = ("eng", "fn", "waits", "incs")

    def __init__(self, eng, fn, waits, incs):
        self.eng = eng
        self.fn = fn
        self.waits = waits
        self.incs = incs


class Prog:
    def __init__(self):
        self.ops = []

    def add(self, eng, fn, deps=(), inc=None, inck=1):
        waits = [d for d in deps if d is not None]
        incs = []
        tok = None
        if inc is not None:
            inc.n += inck
            incs.append((inc, inck))
            tok = (inc, inc.n)
        self.ops.append(Op(eng, fn, waits, incs))
        return tok

    def flush(self, block):
        per = {}
        for op in self.ops:
            per.setdefault(op.eng, []).append(op)
        table = {"tensor": block.tensor, "vector": block.vector,
                 "scalar": block.scalar, "sync": block.sync, "gpsimd": block.gpsimd}
        for name, ops in per.items():
            def body(eng, ops=ops):
                done = {}
                for op in ops:
                    for s, val in op.waits:
                        if done.get(id(s), -1) >= val:
                            continue
                        eng.wait_ge(s.h, val)
                        done[id(s)] = val
                    instr = op.fn(eng)
                    for s, k in op.incs:
                        instr.then_inc(s.h, k)
                        if s.by_order:
                            done[id(s)] = max(done.get(id(s), -1), s.n)
            table[name](body)
        self.ops = []


def build_nc(stage=3, pb_parts=3, pc=6, dbg=False):
    nc = bass.Bass("TRN2", target_bir_lowering=False)
    st = ExitStack()
    ap = lambda t: t.ap()

    x_d = ap(nc.dram_tensor("x", [D, TPC], F32, kind="ExternalInput"))
    y_d = ap(nc.dram_tensor("y", [D, TPC], F32, kind="ExternalOutput"))
    wq_d = ap(nc.dram_tensor("wq", [D, D], F32, kind="ExternalInput"))
    wk_d = ap(nc.dram_tensor("wk", [D, D], F32, kind="ExternalInput"))
    wv_d = ap(nc.dram_tensor("wv", [D, D], F32, kind="ExternalInput"))
    wbo_d = ap(nc.dram_tensor("wbo", [8, 128, 8, 128], F32, kind="ExternalInput"))
    wb1_d = ap(nc.dram_tensor("wb1", [32, 128, 8, 128], F32, kind="ExternalInput"))
    wb2_d = ap(nc.dram_tensor("wb2", [32, 128, 8, 128], F32, kind="ExternalInput"))
    wbh1_d = ap(nc.dram_tensor("wbh1", [32, 128, 32, 128], F32, kind="ExternalInput"))
    wbh2_d = ap(nc.dram_tensor("wbh2", [32, 128, 32, 128], F32, kind="ExternalInput"))
    wbw3_d = ap(nc.dram_tensor("wbw3", [8, 128, 32, 128], F32, kind="ExternalInput"))
    scl_d = ap(nc.dram_tensor("scl", [128, 136], F32, kind="ExternalInput"))
    eye_d = ap(nc.dram_tensor("eye_bf", [128, 128], BF16, kind="ExternalInput"))
    triu_d = ap(nc.dram_tensor("triu_f", [128, 128], BF16, kind="ExternalInput"))
    onesc_d = ap(nc.dram_tensor("ones_bf", [128, 1], BF16, kind="ExternalInput"))
    onesr_d = ap(nc.dram_tensor("onesr_f", [1, 128], F32, kind="ExternalInput"))
    pmask_d = ap(nc.dram_tensor("pmask", [128, 4], F32, kind="ExternalInput"))

    wscr = {k: ap(nc.dram_tensor(f"wscr_{k}", shp, BF16)) for k, shp in [
        ("wo", [8, 128, 8, 128]), ("w1", [32, 128, 8, 128]), ("w2", [32, 128, 8, 128]),
        ("h1", [32, 128, 32, 128]), ("h2", [32, 128, 32, 128]), ("w3", [8, 128, 32, 128])]}
    wsrc = {"wo": wbo_d, "w1": wb1_d, "w2": wb2_d, "h1": wbh1_d, "h2": wbh2_d, "w3": wbw3_d}
    dbg_d = ap(nc.dram_tensor("dbg", [128, 3072], BF16, kind="ExternalOutput")) if dbg else None
    cc_in = ap(nc.dram_tensor("cc_in", [128, 520], F32))
    cc_out = ap(nc.dram_tensor("cc_out", [512, 520], F32))
    rows_d = ap(nc.dram_tensor("rows_scr", [2, TPC], F32))

    def sem(name, by_order=False):
        return Sem(st.enter_context(nc.semaphore(name)), name, by_order)

    s_pe = sem("s_pe", True)
    s_dve = sem("s_dve")
    s_act = sem("s_act")
    s_pool = sem("s_pool")
    s_x = sem("s_x")
    s_wl = [sem(f"s_wl{i}") for i in range(8)]
    s_cst = {k: sem(f"s_c_{k}") for k in wscr}
    s_cc = sem("s_cc")
    s_ccl = sem("s_ccl")
    s_ccin = sem("s_ccin")
    s_out = sem("s_out")
    s_wb = [sem("s_wb0"), sem("s_wb1")]
    s_rwA = sem("s_rwA")
    s_rwB = sem("s_rwB")

    sb = lambda n, shp, dt: st.enter_context(nc.sbuf_tensor(n, shp, dt))
    res = sb("res", [128, IC * TPC], F32)
    rows1 = sb("rows1", [128, 1024], F32)
    rows2 = sb("rows2", [128, 512], F32)
    bfA = sb("bfA", [128, 1024], BF16)
    bfB = sb("bfB", [128, 1024], BF16)
    stg1 = sb("stg1", [128, 1024], F32)
    stg2 = sb("stg2", [128, 1024], F32)
    lnstg1 = sb("lnstg1", [128, 512], F32)
    lnstg2 = sb("lnstg2", [128, 512], F32)
    scl = sb("scl_sb", [128, 136], F32)
    eye_bf = sb("eye_sb", [128, 128], BF16)
    triu_f = sb("triu_sb", [128, 128], BF16)
    ones_bf = sb("ones_sb", [128, 1], BF16)
    onesr_f = sb("onesr_sb", [1, 128], F32)
    pmask = sb("pmask_sb", [128, 4], F32)
    den = sb("den_sb", [128, 8], F32)
    cst = sb("cst_sb", [128, 4], F32)
    nc.const_aps.register(F32, EPS_LN, cst[:, 0:1])
    nc.const_aps.register(F32, -0.5, cst[:, 1:2])
    nc.const_aps.register(F32, 0.0, cst[:, 2:3])
    nc.const_aps.register(F32, 1.0, cst[:, 3:4])

    s1c = lambda og: scl[:, og:og + 1]
    s2c = lambda og: scl[:, 32 + og:33 + og]
    sh1c = lambda og: scl[:, 64 + og:65 + og]
    sh2c = lambda og: scl[:, 96 + og:97 + og]
    s3c = lambda oc: scl[:, 128 + oc:129 + oc]

    ctxAttn = ExitStack()
    vhat = ctxAttn.enter_context(nc.sbuf_tensor("vhat", [128, NCH * (H * 65)], BF16))
    wsb = ctxAttn.enter_context(nc.sbuf_tensor("wsb", [128, 2 * 8192], BF16))
    ktmc = ctxAttn.enter_context(nc.sbuf_tensor("ktmc", [128, 2 * 1024], BF16))
    ST_A = ctxAttn.enter_context(nc.psum_tensor("ST_A", [128, 512], F32))
    ST_B = ctxAttn.enter_context(nc.psum_tensor("ST_B", [128, 512], F32))

    P = Prog()

    G = {"elu": None, "rows": [], "lnstg": None, "cst": None,
         "elu_i": 0, "elu_s": [None, None]}

    def t_mm(out, lhsT, rhs, deps, start=True, stop=True, tp=None):
        return P.add("tensor",
                     lambda e: e.matmul(out, lhsT, rhs, start=start, stop=stop,
                                        tile_position=tp),
                     deps, inc=s_pe)

    def t_tr(out, in_, deps):
        return P.add("tensor", lambda e: e.transpose(out, in_, eye_bf[:, :]),
                     deps, inc=s_pe)

    def v_op(fn, deps):
        return P.add("vector", fn, deps, inc=s_dve)

    def a_op(fn, deps):
        return P.add("scalar", fn, deps, inc=s_act)

    def g_op(fn, deps):
        return P.add("gpsimd", fn, deps, inc=s_pool)

    def dma(eng, out, in_, deps, semobj):
        return P.add(eng, lambda e: e.dma_start(out=out, in_=in_), deps,
                     inc=semobj, inck=16)

    def elu1(ps, out, dep, W, extra=()):
        sl = G["elu_i"] % 2
        G["elu_i"] += 1
        o = sl * 512
        s1 = stg1[:, o:o + W]
        s2 = stg2[:, o:o + W]
        deps = [dep, G["elu_s"][sl]] + list(extra)
        a1 = a_op(lambda e: e.activation(s1, ps, AF.Relu), deps)
        v1 = v_op(lambda e: e.tensor_tensor(s2, ps, s1, ALU.subtract), [a1])
        a2 = a_op(lambda e: e.activation(s2, s2, AF.Exp), [v1])
        v2 = v_op(lambda e: e.tensor_tensor(out, s1, s2, ALU.add), [a2])
        G["elu_s"][sl] = v2
        return v2

    def ln_rows(W, sx, sq, bc0, bc1, dep_stats, save_g=None):
        mu = rows1[0:1, 0:W]
        tmp = rows1[0:1, 512:512 + W]
        var = rows2[0:1, 0:W]
        r_ = var
        mr = mu
        rw = list(G["rows"]) + ([G["lnstg"]] if G["lnstg"] else [])
        t1 = v_op(lambda e: e.tensor_scalar(mu, sx, 1.0 / D, None, ALU.mult), [dep_stats] + rw)
        t2 = v_op(lambda e: e.tensor_scalar(var, sq, 1.0 / D, None, ALU.mult), [dep_stats] + rw)
        t3 = v_op(lambda e: e.tensor_tensor(tmp, mu, mu, ALU.mult), [t1])
        t4 = v_op(lambda e: e.tensor_tensor(var, var, tmp, ALU.subtract), [t2, t3])
        a1 = a_op(lambda e: e.activation(r_, var, AF.Ln, bias=EPS_LN), [t4, G["cst"]] + rw)
        a2 = a_op(lambda e: e.activation(r_, r_, AF.Exp, scale=-0.5), [a1])
        bc_r = t_mm(bc0, onesr_f[:, :], r_, [a2] + rw)
        t5 = v_op(lambda e: e.tensor_tensor(mr, mu, r_, ALU.mult), [a2])
        bc_m = t_mm(bc1, onesr_f[:, :], mr, [t5])
        G["rows"] = [bc_r, bc_m, t5]
        if save_g is not None:
            t0s = save_g * 512
            sr = dma("scalar", rows_d[0:1, t0s:t0s + 512], r_, [a2], s_rwA)
            sm = dma("scalar", rows_d[1:2, t0s:t0s + 512], mr, [t5], s_rwA)
            G["rows"] += [sr, sm]
        return bc_r, bc_m

    def ln_stats(W, src_f32, sx, sq, extra_dep, sq_dep=None):
        m1 = m2 = None
        prev = []
        for ic in range(IC):
            o = (ic % 2) * 512
            b_x = bfA[:, o:o + W]
            b_q = bfB[:, o:o + W]
            war = [prev[-2]] if len(prev) >= 2 else []
            dep = ([extra_dep] if extra_dep else []) + war
            tc = v_op(lambda e, o=b_x, i=src_f32(ic): e.tensor_copy(o, i), dep)
            tq = v_op(lambda e, o=b_q, i=b_x: e.tensor_tensor(o, i, i, ALU.mult), [tc])
            m1 = t_mm(sx, ones_bf[:, :], b_x, [tc], start=(ic == 0), stop=(ic == IC - 1))
            m2 = t_mm(sq, ones_bf[:, :], b_q,
                      [tq] + ([sq_dep] if (ic == 0 and sq_dep) else []),
                      start=(ic == 0), stop=(ic == IC - 1))
            prev.append(m2)
        return m2, m1

    def ln_apply(W, src_f32, out_bf, bc0, bc1, bc_toks, n=IC):
        toks = []
        for ic in range(n):
            dep = [bc_toks[0], bc_toks[1], G["lnstg"]] + ([toks[-1]] if toks else [])
            p1 = v_op(lambda e, i=src_f32(ic): e.tensor_tensor(lnstg1[:, 0:W], i, bc0, ALU.mult), dep)
            p2 = v_op(lambda e, o=out_bf(ic): e.tensor_tensor(o, lnstg1[:, 0:W], bc1, ALU.subtract), [p1])
            toks.append(p2)
        G["lnstg"] = toks[-1]
        return toks

    ctxA = ExitStack()
    sbA = lambda n, shp, dt: ctxA.enter_context(nc.sbuf_tensor(n, shp, dt))
    psA = lambda n, shp, dt=F32: ctxA.enter_context(nc.psum_tensor(n, shp, dt))
    wstg = sbA("wstg", [128, 2048], F32)
    xlnA = sbA("xlnA", [128, IC * 512], BF16)
    sst = sbA("sst", [128, 520], F32)
    PR0 = psA("PR0", [128, 512])
    PR1 = psA("PR1", [128, 512])
    BC0 = psA("BC0", [128, 512])
    BC1 = psA("BC1", [128, 512])
    STT = psA("STT", [1, 1024])

    c1 = v_op(lambda e: e.memset(cst[:, 0:1], EPS_LN), [])
    c2 = v_op(lambda e: e.memset(cst[:, 1:2], -0.5), [c1])
    c3 = v_op(lambda e: e.memset(cst[:, 2:3], 0.0), [c2])
    G["cst"] = v_op(lambda e: e.memset(cst[:, 3:4], 1.0), [c3])
    for ic in range(IC):
        dma("sync", res[:, ic * TPC:(ic + 1) * TPC], x_d[ic * 128:(ic + 1) * 128, :], [], s_x)
    dma("sync", scl[:, :], scl_d, [], s_x)
    dma("sync", eye_bf[:, :], eye_d, [], s_x)
    dma("sync", triu_f[:, :], triu_d, [], s_x)
    dma("sync", ones_bf[:, :], onesc_d, [], s_x)
    dma("sync", onesr_f[:, :], onesr_d, [], s_x)
    dma("sync", pmask[:, :], pmask_d, [], s_x)
    tok_x = (s_x, s_x.n)

    def load_w_cast(w_dram, slot_base, prev):
        casts = list(prev)
        for ic in range(IC):
            half = (ic % 2) * 1024
            dep = [casts[-2]] if len(casts) >= 2 else []
            ld = dma("sync", wstg[:, half:half + 1024],
                     w_dram[ic * 128:(ic + 1) * 128, :], dep, s_wb[ic % 2])
            c = v_op(lambda e, o=wsb[:, slot_base + ic * 1024: slot_base + (ic + 1) * 1024],
                     i=wstg[:, half:half + 1024]: e.tensor_copy(o, i), [ld])
            casts.append(c)
        return casts[-1], casts[-2:]

    tok_wk, chain = load_w_cast(wk_d, 0, [])
    tok_wv, chain = load_w_cast(wv_d, 8192, chain)

    cast_toks = {}
    first_cast = True
    for name in ["wo", "w1", "w2", "h1", "h2", "w3"]:
        n = {"wo": 8, "w1": 32, "w2": 32, "h1": 32, "h2": 32, "w3": 8}[name]
        for og in range(n):
            deps = [tok_x, tok_wk, tok_wv] if first_cast else []
            first_cast = False
            dma("gpsimd", wscr[name][og], wsrc[name][og], deps, s_cst[name])
        cast_toks[name] = (s_cst[name], s_cst[name].n)

    vhat_ones = vhat.ap().rearrange("p (c h o) -> p (c h) o", c=NCH, h=H)[:, :, 64:65]
    tok_vones = v_op(lambda e: e.memset(vhat_ones, 1.0), [])

    pr_war = {0: None, 1: None}
    state_last = None
    k_ev_last = None
    prev_group_tail = tok_x
    tok_sst0 = v_op(lambda e: e.memset(sst[:, :], 0.0), [])
    sst_add = tok_sst0
    for g in range(4):
        t0 = g * 512
        src = lambda ic, t0=t0: res[:, ic * TPC + t0: ic * TPC + t0 + 512]
        xo = lambda ic: xlnA[:, ic * 512:(ic + 1) * 512]
        stats_dep = prev_group_tail
        m_last, _ = ln_stats(512, src, STT[0:1, 0:512], STT[0:1, 512:1024], stats_dep)
        bc = ln_rows(512, STT[0:1, 0:512], STT[0:1, 512:1024], BC0[:, :], BC1[:, :], m_last,
                     save_g=g)
        ap_toks = ln_apply(512, src, xo, BC0[:, :], BC1[:, :], bc)
        for cc4 in range(4):
            c = g * 4 + cc4
            v_ev = {}
            for eh in range(2):
                pr = PR0 if eh == 0 else PR1
                mmt = None
                for ic in range(IC):
                    lhs = xlnA[:, ic * 512 + cc4 * 128: ic * 512 + cc4 * 128 + 128]
                    rhs = wsb[:, 8192 + ic * 1024 + eh * 512: 8192 + ic * 1024 + eh * 512 + 512]
                    deps = [ap_toks[ic], tok_wv] if ic == 0 else []
                    if ic == 0:
                        deps.append(pr_war[eh])
                    else:
                        deps = [ap_toks[ic]]
                    mmt = t_mm(pr[:, :], lhs, rhs, deps, start=(ic == 0), stop=(ic == IC - 1))
                vv = vhat.ap().rearrange("p (c h o) -> p c h o", c=NCH, h=H)[:, c, eh * 8:(eh + 1) * 8, 0:64]
                prv = pr.ap().rearrange("p (h o) -> p h o", h=8)
                ev = v_op(lambda e, o=vv, i=prv: e.tensor_copy(o, i), [mmt, tok_vones])
                pr_war[eh] = ev
                v_ev[eh] = ev
            kslot = ktmc[:, (c % 2) * 1024:(c % 2) * 1024 + 1024]
            k_ev = {}
            for eh in range(2):
                pr = PR0 if eh == 0 else PR1
                mmt = None
                for ic in range(IC):
                    lhs = xlnA[:, ic * 512 + cc4 * 128: ic * 512 + cc4 * 128 + 128]
                    rhs = wsb[:, ic * 1024 + eh * 512: ic * 1024 + eh * 512 + 512]
                    deps = [ap_toks[ic], tok_wk, pr_war[eh]] if ic == 0 else [ap_toks[ic]]
                    mmt = t_mm(pr[:, :], lhs, rhs, deps, start=(ic == 0), stop=(ic == IC - 1))
                ev = elu1(pr[:, :], kslot[:, eh * 512:eh * 512 + 512], mmt, 512,
                          extra=[state_last])
                pr_war[eh] = ev
                k_ev[eh] = ev
            k_ev_last = k_ev[1]
            for h in range(H):
                lhs = kslot[:, h * 64:(h + 1) * 64]
                rhs = vhat[:, c * (H * 65) + h * 65: c * (H * 65) + (h + 1) * 65]
                stp = ST_A if (h // 2) % 2 == 0 else ST_B
                pbase = 64 * (h % 2)
                slot = h // 4
                out = stp[pbase:pbase + 64, slot * 65:(slot + 1) * 65]
                tp = (0, pbase) if pbase else None
                state_last = t_mm(out, lhs, rhs,
                                  [k_ev[h // 8], v_ev[h // 8], sst_add],
                                  start=True, stop=True, tp=tp)
            vA = bass.AP(sst, 0, [[520, 128], [130, 4], [1, 65]])
            vB = bass.AP(sst, 65, [[520, 128], [130, 4], [1, 65]])
            sA = bass.AP(ST_A, 0, [[512, 128], [65, 4], [1, 65]])
            sB = bass.AP(ST_B, 0, [[512, 128], [65, 4], [1, 65]])
            a1_ = v_op(lambda e: e.tensor_tensor(vA, vA, sA, ALU.add), [state_last])
            sst_add = v_op(lambda e: e.tensor_tensor(vB, vB, sB, ALU.add), [state_last])
        prev_group_tail = state_last

    d1 = dma("sync", cc_in, sst[:, :], [sst_add], s_ccin)
    cc_tok = P.add("gpsimd",
                   lambda e: e.collective_compute(
                       "AllGather", ALU.bypass,
                       replica_groups=[[0, 1, 2, 3], [4, 5, 6, 7]],
                       ins=[cc_in], outs=[cc_out]),
                   [d1], inc=s_cc)

    if stage < 2:
        P.add("gpsimd", lambda e: e.memset(den[:, 0:1], 0.0), [(s_cc, s_cc.n)])
    with nc.Block("pA", no_gpsimd_drain=True) as blkA:
        P.flush(blkA)
    ctxA.close()
    if stage < 2:
        ctxAttn.close()
        st.close()
        return nc

    ctxB = ExitStack()
    sbB = lambda n, shp, dt: ctxB.enter_context(nc.sbuf_tensor(n, shp, dt))
    psB = lambda n, shp, dt=F32: ctxB.enter_context(nc.psum_tensor(n, shp, dt))
    xlnB = sbB("xlnB", [128, 2 * IC * 256], BF16)
    qst = sbB("qst", [128, 2 * 8 * 256], BF16)
    kfm = sbB("kfm", [128, 1024], BF16)
    atm = sbB("atm", [128, 2 * 512], BF16)
    ytm = sbB("ytm", [128, 1024], BF16)
    yfm = sbB("yfm", [128, 2 * 8 * 256], BF16)
    wos = sbB("wos", [128, 2 * 1024], BF16)
    aof = sbB("aof", [128, 8 * 256], F32)
    aob = sbB("aob", [128, 8 * 256], BF16)
    sinit = sbB("sinit", [128, 520], F32)
    ccg = sbB("ccg", [128, 520], F32)
    ssb = sbB("ssb", [128, 520], BF16)

    PS_tr = psB("PS_tr", [128, 1024], BF16)
    PS_at = psB("PS_at", [128, 512])
    PS_y = psB("PS_y", [128, 260])
    PR0b = psB("PR0b", [128, 512])
    PR1b = psB("PR1b", [128, 512])
    PSm = psB("PSm", [128, 512])

    wq_casts = list(chain)
    for ic in range(IC):
        half = (ic % 2) * 1024
        dep = [wq_casts[-2]] if len(wq_casts) >= 2 else []
        ld = dma("sync", aof[:, half:half + 1024], wq_d[ic * 128:(ic + 1) * 128, :], dep, s_wb[ic % 2])
        c = v_op(lambda e, o=wsb[:, 8192 + ic * 1024: 8192 + (ic + 1) * 1024],
                 i=aof[:, half:half + 1024]: e.tensor_copy(o, i), [ld])
        wq_casts.append(c)
    tok_wq = wq_casts[-1]

    t_ms = v_op(lambda e: e.memset(sinit[:, :], 0.0), [])
    acc = t_ms
    for r in range(4):
        ld = dma("sync", ccg[:, :], cc_out[r * 128:(r + 1) * 128, :], [(s_cc, s_cc.n), acc], s_ccl)
        acc = v_op(lambda e, m=pmask[:, r:r + 1]: e.scalar_tensor_tensor(
            sinit[:, :], ccg[:, :], m, sinit[:, :], ALU.mult, ALU.add), [ld, acc])
    tok_sinit = acc

    pr_warB = {0: tok_wq, 1: tok_wq}
    state_lastB = None
    state_addB = None
    ssb_tok = None
    pa_tail = sst_add
    PB = {"psat": None}
    rowld = {}
    kev_pend = {}

    def emit_rowload(g2l, deps):
        if g2l >= 8:
            return
        t0l = g2l * 256
        l1 = dma("sync", rows2[0:1, 256:512], rows_d[0:1, t0l:t0l + 256],
                 deps + [(s_rwA, 16 * 8)], s_rwB)
        l2 = dma("sync", rows1[0:1, 768:1024], rows_d[1:2, t0l:t0l + 256],
                 deps, s_rwB)
        rowld[g2l] = (l1, l2)
    ytr_war = None
    kf_war = None
    y_div_last = None
    ln2_last = None
    wo_war = {0: None, 1: None}
    yfm_war = {0: None, 1: None}
    qst_war = {0: None, 1: None}
    xln_war = {0: None, 1: None}
    aof_war = tok_wq
    k_ev_lastB = None

    for g2 in range(8):
        t0 = g2 * 256
        slot = g2 % 2
        src = lambda ic, t0=t0: res[:, ic * TPC + t0: ic * TPC + t0 + 256]
        xo = lambda ic, slot=slot: xlnB[:, slot * IC * 256 + ic * 256: slot * IC * 256 + (ic + 1) * 256]
        if g2 == 0:
            emit_rowload(0, [])
        bc_deps = [rowld[g2][0], rowld[g2][1], PB["psat"]]
        if g2 == 0:
            bc_deps.append(pa_tail)
        bc_r = t_mm(PS_at[:, 0:256], onesr_f[:, :], rows2[0:1, 256:512], bc_deps)
        bc_m = t_mm(PS_at[:, 256:512], onesr_f[:, :], rows1[0:1, 768:1024], [])
        bc = (bc_r, bc_m)
        ap_toks = ln_apply(256, src, xo, PS_at[:, 0:256], PS_at[:, 256:512], bc)
        xln_done = ap_toks
        emit_rowload(g2 + 1, [bc_r, bc_m])
        q_toks = [None] * 8
        for p in range(4):
            pr = PR0b if p % 2 == 0 else PR1b
            mmt = None
            for half in range(2):
                dc = 2 * p + half
                for ic in range(IC):
                    lhsT = wsb[:, 8192 + ic * 1024 + dc * 128: 8192 + ic * 1024 + dc * 128 + 128]
                    rhs = xlnB[:, slot * IC * 256 + ic * 256: slot * IC * 256 + (ic + 1) * 256]
                    deps = [xln_done[ic]]
                    if ic == 0 and half == 0:
                        deps += [tok_wq, pr_warB[p % 2]]
                    mmt = t_mm(pr[:, half * 256:half * 256 + 256], lhsT, rhs, deps,
                               start=(ic == 0), stop=(ic == IC - 1))
            qo = qst[:, slot * 2048 + 2 * p * 256: slot * 2048 + (2 * p + 2) * 256]
            ev = elu1(pr[:, 0:512], qo, mmt, 512, extra=[qst_war[slot], state_lastB])
            pr_warB[p % 2] = ev
            q_toks[2 * p] = q_toks[2 * p + 1] = ev
        qst_war[slot] = q_toks[-1]
        def emit_kproj(c, xd):
            cil = c % 2
            slotl = (c // 2) % 2
            kslot_l = ktmc[:, (c % 2) * 1024:(c % 2) * 1024 + 1024]
            kev = {}
            for eh in range(2):
                pr = PR0b if eh == 0 else PR1b
                mmt = None
                for ic in range(IC):
                    lhsT = xlnB[:, slotl * IC * 256 + ic * 256 + cil * 128: slotl * IC * 256 + ic * 256 + cil * 128 + 128]
                    rhs = wsb[:, ic * 1024 + eh * 512: ic * 1024 + eh * 512 + 512]
                    deps = [xd[ic]]
                    if ic == 0:
                        deps += [tok_wk, pr_warB[eh]]
                    mmt = t_mm(pr[:, :], lhsT, rhs, deps, start=(ic == 0), stop=(ic == IC - 1))
                ev = elu1(pr[:, :], kslot_l[:, eh * 512:eh * 512 + 512], mmt, 512,
                          extra=[state_lastB])
                pr_warB[eh] = ev
                kev[eh] = ev
            kev_pend[c] = kev

        for ci in (range(2) if pb_parts >= 2 else []):
            c = 2 * g2 + ci
            kslot = ktmc[:, (c % 2) * 1024:(c % 2) * 1024 + 1024]
            if c not in kev_pend:
                emit_kproj(c, xln_done)
            k_ev = kev_pend.pop(c)
            k_ev_lastB = k_ev[1]
            ktr = []
            for j in range(8):
                deps = [k_ev[j // 4]]
                if ytr_war is not None and j == 0:
                    deps.append(ytr_war)
                ktr.append(t_tr(PS_tr[:, j * 128:(j + 1) * 128],
                                kslot[:, j * 128:(j + 1) * 128], deps))
            kf = kfm[:, 0:1024]
            kf_tok = v_op(lambda e, o=kf: e.tensor_copy(o, PS_tr[:, :]),
                          ktr + [y_div_last])
            if pc < 2:
                ytr_war = kf_tok
                continue
            sdep = [tok_sinit if c == 0 else state_addB]
            ssb_tok = v_op(lambda e: e.tensor_copy(ssb[:, :], sinit[:, :]), sdep)
            if pc < 3:
                ytr_war = ssb_tok
                continue
            waves = [[0, 2, 4, 6], [8, 10, 12, 14], [1, 3, 5, 7], [9, 11, 13, 15]]
            wave_mk = {}
            atm_war = {0: None, 1: None}
            at_war = None

            def emit_at(w):
                nonlocal at_war
                heads = waves[w]
                pb = 64 * (heads[0] % 2)
                at_toks = []
                for j, h in enumerate(heads):
                    lhsT = kf[pb:pb + 64, (h // 2) * 128:(h // 2) * 128 + 128]
                    rhs = qst[pb:pb + 64,
                              slot * 2048 + (h // 2) * 256 + ci * 128: slot * 2048 + (h // 2) * 256 + ci * 128 + 128]
                    deps = [kf_tok, q_toks[h // 2]]
                    if j == 0:
                        deps.append(at_war)
                        deps.append(xln_done[-1])
                    at_toks.append(t_mm(PS_at[:, j * 128:(j + 1) * 128], lhsT, rhs, deps))
                if pc == 36 or (pc == 38 and w == 1):
                    at_war = at_toks[-1]
                    wave_mk[w] = at_toks
                    return
                am = atm[:, (w % 2) * 512:(w % 2) * 512 + 512]
                mks = []
                for j, h in enumerate(heads):
                    mks.append(v_op(lambda e, j=j, am=am: e.tensor_tensor(
                        am[:, j * 128:(j + 1) * 128], PS_at[:, j * 128:(j + 1) * 128],
                        triu_f[:, :], ALU.mult), [at_toks[-1], atm_war[w % 2]]))
                wave_mk[w] = mks
                at_war = mks[-1]
                PB["psat"] = mks[-1]

            def emit_y(w):
                nonlocal y_div_last
                heads = waves[w]
                pb = 64 * (heads[0] % 2)
                am = atm[:, (w % 2) * 512:(w % 2) * 512 + 512]
                mks = wave_mk[w]
                ymm = []
                for j, h in enumerate(heads):
                    m1 = t_mm(PS_y[:, j * 65:(j + 1) * 65],
                              am[:, j * 128:(j + 1) * 128],
                              vhat[:, c * (H * 65) + h * 65: c * (H * 65) + (h + 1) * 65],
                              [mks[j]] + ([y_div_last] if j == 0 else []),
                              start=True, stop=False)
                    m2 = t_mm(PS_y[:, j * 65:(j + 1) * 65],
                              qst[pb:pb + 64, slot * 2048 + (h // 2) * 256 + ci * 128: slot * 2048 + (h // 2) * 256 + ci * 128 + 128],
                              ssb[pb:pb + 64, 65 * (h // 2):65 * (h // 2) + 65],
                              [ssb_tok], start=False, stop=True)
                    ymm.append(m2)
                atm_war[w % 2] = ymm[-1]
                dn = bass.AP(PS_y, 64, [[260, 128], [65, 4]])
                d0 = v_op(lambda e: e.tensor_scalar(den[:, 0:4], dn, EPS_ATT, None, ALU.add), ymm)
                d1_ = v_op(lambda e: e.reciprocal(den[:, 4:8], den[:, 0:4]), [d0])
                for j, h in enumerate(heads):
                    y_div_last = v_op(lambda e, j=j, h=h: e.tensor_scalar(
                        ytm[:, h * 64:(h + 1) * 64],
                        PS_y[:, j * 65:j * 65 + 64], den[:, 4 + j:5 + j], None, ALU.mult), [d1_])

            def pe_fence():
                v = s_pe.n
                P.add("tensor", lambda e, v=v: e.wait_ge(s_pe.h, v), [])

            def dbg_dump(src_ap, off, deps, eng="sync"):
                n = src_ap.free_size()
                sem_ = s_out if eng == "sync" else s_cst["wo"]
                return dma(eng, bass.AP(dbg_d.tensor, off, [[3072, 128], [1, n]]),
                           src_ap, deps, sem_)

            emit_at(0)
            if pc not in (36, 37):
                emit_at(1)
            if pc >= 5 and pc not in (36, 37, 38):
                emit_y(0)
                emit_y(1)
            if ci == 0:
                emit_kproj(c + 1, xln_done)
            if dbg and c == 0:
                t1 = dbg_dump(atm[:, 0:512], 0, [wave_mk[0][-1], atm_war[0]])
                t2 = dbg_dump(atm[:, 512:1024], 512, [wave_mk[1][-1], atm_war[1]])
                atm_war[0] = t1
                atm_war[1] = t2
            if pc not in (35, 36, 37, 38):
                pe_fence()
                emit_at(2)
                emit_at(3)
                pe_fence()
                if pc >= 5:
                    emit_y(2)
                    emit_y(3)
            if pc < 5:
                y_div_last = at_war
            if pc < 6:
                state_addB = y_div_last
                ytr_war = y_div_last
                continue
            if dbg and c == 0:
                dbg_ytm = dbg_dump(ytm[:, 0:1024], 1024, [y_div_last])
                dbg_dump(den[:, 0:8], 2048, [y_div_last], eng="gpsimd")
                dbg_dump(ssb[:, 0:520], 2100, [ssb_tok])
            for h in range(H):
                lhsT = kslot[:, h * 64:(h + 1) * 64]
                rhs = vhat[:, c * (H * 65) + h * 65: c * (H * 65) + (h + 1) * 65]
                stp = ST_A if (h // 2) % 2 == 0 else ST_B
                pbase = 64 * (h % 2)
                slot_h = h // 4
                out = stp[pbase:pbase + 64, slot_h * 65:(slot_h + 1) * 65]
                tp = (0, pbase) if pbase else None
                state_lastB = t_mm(out, lhsT, rhs,
                                   [ssb_tok, k_ev[h // 8], state_addB],
                                   start=True, stop=True, tp=tp)
            if c < NCH - 1:
                vA = bass.AP(sinit, 0, [[520, 128], [130, 4], [1, 65]])
                vB = bass.AP(sinit, 65, [[520, 128], [130, 4], [1, 65]])
                sA = bass.AP(ST_A, 0, [[512, 128], [65, 4], [1, 65]])
                sB = bass.AP(ST_B, 0, [[512, 128], [65, 4], [1, 65]])
                v_op(lambda e, vA=vA, sA=sA: e.tensor_tensor(vA, vA, sA, ALU.add), [state_lastB])
                state_addB = v_op(lambda e, vB=vB, sB=sB: e.tensor_tensor(vB, vB, sB, ALU.add), [state_lastB])
            ytr = []
            for j in range(8):
                deps = [y_div_last, kf_tok] if j == 0 else [y_div_last]
                ytr.append(t_tr(PS_tr[:, j * 128:(j + 1) * 128],
                                ytm[:, j * 128:(j + 1) * 128],
                                deps))
            yf_dst = bass.AP(yfm, slot * 2048 + ci * 128, [[4096, 128], [256, 8], [1, 128]])
            ptr_src = PS_tr.ap().rearrange("p (j t) -> p j t", j=8)
            ytr_war = v_op(lambda e, o=yf_dst, i=ptr_src: e.tensor_copy(o, i),
                           ytr + [yfm_war[slot]])
            yfm_done = ytr_war
        if pb_parts < 3:
            continue
        wo_evac = []
        wo_mm_last = None
        for oc in range(8):
            wslot = wos[:, (oc % 2) * 1024:(oc % 2) * 1024 + 1024]
            dep_w = [cast_toks["wo"]] + ([wo_evac[-2]] if len(wo_evac) >= 2 else [wo_war[oc % 2]])
            ld = dma("sync", wslot, wscr["wo"][oc].rearrange("p a b -> p (a b)"), dep_w, s_wl[oc % 2])
            pr = PR0b if oc % 2 == 0 else PR1b
            mmt = None
            for ec in range(8):
                deps = [ld, yfm_done] if ec == 0 else []
                if ec == 0:
                    deps.append(pr_warB[oc % 2])
                mmt = t_mm(pr[:, 0:256], wslot[:, ec * 128:(ec + 1) * 128],
                           yfm[:, slot * 2048 + ec * 256: slot * 2048 + (ec + 1) * 256],
                           deps, start=(ec == 0), stop=(ec == 7))
            wo_mm_last = mmt
            a1 = a_op(lambda e, oc=oc, pr=pr: e.activation(
                aof[:, oc * 256:(oc + 1) * 256], pr[:, 0:256], AF.Copy),
                [mmt, aof_war])
            v1 = v_op(lambda e, oc=oc, pr=pr: e.tensor_copy(
                aob[:, oc * 256:(oc + 1) * 256], pr[:, 0:256]), [mmt, a1])
            pr_warB[oc % 2] = v1
            wo_evac.append(v1)
        yfm_war[slot] = wo_mm_last
        wo_war[0] = wo_evac[-2]
        wo_war[1] = wo_evac[-1]
        sx = PSm[0:1, 0:256]
        sq = PR0b[0:1, 256:512]
        m1 = m2 = None
        m2s = []
        for oc in range(8):
            o = (oc % 2) * 512
            war = [m2s[-2]] if len(m2s) >= 2 else []
            tq = v_op(lambda e, oc=oc, o=o: e.tensor_tensor(
                bfB[:, o:o + 256], aob[:, oc * 256:(oc + 1) * 256],
                aob[:, oc * 256:(oc + 1) * 256], ALU.mult), [wo_evac[oc]] + war)
            deps1 = [wo_evac[oc]] + ((G["rows"] + [G["lnstg"]]) if oc == 0 else [])
            m1 = t_mm(sx, ones_bf[:, :], aob[:, oc * 256:(oc + 1) * 256], deps1,
                      start=(oc == 0), stop=(oc == 7))
            m2 = t_mm(sq, ones_bf[:, :], bfB[:, o:o + 256],
                      [tq] + ([wo_evac[-2]] if oc == 0 else []),
                      start=(oc == 0), stop=(oc == 7))
            m2s.append(m2)
        ln2_m2_last = m2
        bc = ln_rows(256, sx, sq, PSm[:, 0:256], PSm[:, 256:512], m2)
        for oc in range(8):
            dep = [bc[0], bc[1], G["lnstg"]] + ([ln2_last] if ln2_last else [])
            p1 = v_op(lambda e, oc=oc: e.tensor_tensor(
                lnstg1[:, 0:256], aof[:, oc * 256:(oc + 1) * 256], PSm[:, 0:256], ALU.mult), dep)
            p2 = v_op(lambda e: e.tensor_tensor(lnstg2[:, 0:256], lnstg1[:, 0:256],
                                                PSm[:, 256:512], ALU.subtract), [p1])
            p3 = g_op(lambda e, oc=oc, t0=t0: e.tensor_tensor(
                res[:, oc * TPC + t0: oc * TPC + t0 + 256],
                res[:, oc * TPC + t0: oc * TPC + t0 + 256], lnstg2[:, 0:256], ALU.add), [p2])
            G["lnstg"] = p2
            ln2_last = p3
        aof_war = ln2_last

    if stage < 3:
        tail = ln2_last if pb_parts >= 3 else (
            (state_addB if state_addB else ytr_war) if pb_parts >= 2 else qst_war[1])
        P.add("gpsimd", lambda e: e.memset(den[:, 0:1], 0.0),
              [tail, (s_pe, s_pe.n), (s_dve, s_dve.n), (s_act, s_act.n)])
    with nc.Block("pB", no_gpsimd_drain=True) as blkB:
        P.flush(blkB)
    ctxB.close()
    ctxAttn.close()
    if stage < 3:
        st.close()
        return nc

    ctxF = ExitStack()
    sbF = lambda n, shp, dt: ctxF.enter_context(nc.sbuf_tensor(n, shp, dt))
    psF = lambda n, shp, dt=F32: ctxF.enter_context(nc.psum_tensor(n, shp, dt))
    hbuf = sbF("hbuf", [128, HC * 512], BF16)
    gbuf = sbF("gbuf", [128, HC * 512], BF16)
    xln3 = sbF("xln3", [128, IC * 512], BF16)
    wbig = sbF("wbig", [128, 2 * 4096], BF16)
    wsml = sbF("wsml", [128, 4 * 1024], BF16)
    silu_s = sbF("silu_s", [128, 512], BF16)
    y3f = sbF("y3f", [128, 8 * 512], F32)

    PRf = [psF(f"PRf{i}", [128, 512]) for i in range(4)]
    PSmf = psF("PSmf", [1, 1024])
    BCf0 = psF("BCf0", [128, 512])
    BCf1 = psF("BCf1", [128, 512])

    out_count = 0
    prev_tail = ln2_last
    wslot_war = {k: None for k in ["s0", "s1", "s2", "s3", "b0", "b1"]}
    for s in range(NSL):
        t0 = s * TS
        src = lambda ic, t0=t0: res[:, ic * TPC + t0: ic * TPC + t0 + TS]
        xo = lambda ic: xln3[:, ic * 512:(ic + 1) * 512]
        m_last, _ = ln_stats(512, src, PSmf[0:1, 0:512], PSmf[0:1, 512:1024], prev_tail)
        bc = ln_rows(512, PSmf[0:1, 0:512], PSmf[0:1, 512:1024], PRf[2][:, :], PRf[3][:, :], m_last)
        xln_done = ln_apply(512, src, xo, PRf[2][:, :], PRf[3][:, :], bc)
        h_toks = []
        for og in range(HC):
            sl = og % 2
            w1b = wsml[:, sl * 1024: sl * 1024 + 1024]
            w2b = wsml[:, 2048 + sl * 1024: 2048 + sl * 1024 + 1024]
            pr_war_ = [h_toks[-2]] if len(h_toks) >= 2 else [xln_done[-1]]
            ld_eng = "gpsimd" if (s == 0 and og < 2) else "sync"
            l1 = dma(ld_eng, w1b, wscr["w1"][og].rearrange("p a b -> p (a b)"),
                     [cast_toks["w1"], wslot_war["s" + str(sl)]], s_wl[2 + sl])
            l2 = dma(ld_eng, w2b, wscr["w2"][og].rearrange("p a b -> p (a b)"),
                     [cast_toks["w2"], wslot_war["s" + str(2 + sl)]], s_wl[4 + sl])
            pra = PRf[sl * 2]
            prb = PRf[sl * 2 + 1]
            ma = mb = None
            for ic in range(IC):
                deps = ([l1, xln_done[ic]] + pr_war_) if ic == 0 else [xln_done[ic]]
                ma = t_mm(pra[:, :], w1b[:, ic * 128:(ic + 1) * 128], xo(ic),
                          deps, start=(ic == 0), stop=(ic == IC - 1))
            wslot_war["s" + str(sl)] = ma
            sa = a_op(lambda e, pra=pra, og=og: e.activation(
                silu_s[:, :], pra[:, :], AF.Silu, scale=s1c(og)),
                [ma, G["elu"]] + ([h_toks[-1]] if h_toks else []))
            G["elu"] = sa
            for ic in range(IC):
                deps = ([l2, xln_done[ic]] + pr_war_) if ic == 0 else [xln_done[ic]]
                mb = t_mm(prb[:, :], w2b[:, ic * 128:(ic + 1) * 128], xo(ic),
                          deps, start=(ic == 0), stop=(ic == IC - 1))
            wslot_war["s" + str(2 + sl)] = mb
            hv = v_op(lambda e, prb=prb, og=og: e.scalar_tensor_tensor(
                hbuf[:, og * 512:(og + 1) * 512], prb[:, :], s2c(og), silu_s[:, :],
                ALU.mult, ALU.mult), [mb, sa])
            h_toks.append(hv)
        g_toks = []
        for og in range(HC):
            sl = og % 2
            wb = wbig[:, sl * 4096: sl * 4096 + 4096]
            ld_eng = "gpsimd" if (s == 0 and og < 2) else "sync"
            ld = dma(ld_eng, wb, wscr["h1"][og].rearrange("p a b -> p (a b)"),
                     [cast_toks["h1"], wslot_war["b" + str(sl)]], s_wl[6 + sl])
            pr = PRf[og % 3]
            mm = None
            for icp in range(HC):
                deps = [ld, h_toks[-1]] + ([g_toks[-3]] if len(g_toks) >= 3 else [])
                mm = t_mm(pr[:, :], wb[:, icp * 128:(icp + 1) * 128],
                          hbuf[:, icp * 512:(icp + 1) * 512],
                          deps if icp == 0 else [],
                          start=(icp == 0), stop=(icp == HC - 1))
            wslot_war["b" + str(sl)] = mm
            gv = a_op(lambda e, pr=pr, og=og: e.activation(
                gbuf[:, og * 512:(og + 1) * 512], pr[:, :], AF.Gelu, scale=sh1c(og)), [mm])
            g_toks.append(gv)
        u_toks = []
        for og in range(HC):
            sl = og % 2
            wb = wbig[:, sl * 4096: sl * 4096 + 4096]
            ld = dma("sync", wb, wscr["h2"][og].rearrange("p a b -> p (a b)"),
                     [cast_toks["h2"], wslot_war["b" + str(sl)]], s_wl[6 + sl])
            pr = PRf[og % 3]
            mm = None
            for icp in range(HC):
                deps = [ld, g_toks[-1]] + ([u_toks[-3]] if len(u_toks) >= 3 else [])
                mm = t_mm(pr[:, :], wb[:, icp * 128:(icp + 1) * 128],
                          gbuf[:, icp * 512:(icp + 1) * 512],
                          deps if icp == 0 else [],
                          start=(icp == 0), stop=(icp == HC - 1))
            wslot_war["b" + str(sl)] = mm
            uv = a_op(lambda e, pr=pr, og=og: e.activation(
                hbuf[:, og * 512:(og + 1) * 512], pr[:, :], AF.Gelu, scale=sh2c(og)), [mm])
            u_toks.append(uv)
        y3_toks = []
        stat_toks = []
        sx = PSmf[0:1, 0:512]
        sq = PSmf[0:1, 512:1024]
        m1 = m2 = None
        for oc in range(8):
            sl = oc % 2
            wb = wbig[:, sl * 4096: sl * 4096 + 4096]
            ld = dma("sync", wb, wscr["w3"][oc].rearrange("p a b -> p (a b)"),
                     [cast_toks["w3"], wslot_war["b" + str(sl)]], s_wl[6 + sl])
            pr = PRf[oc % 3]
            mm = None
            for icp in range(HC):
                deps = [ld, u_toks[-1]] + ([y3_toks[-3]] if len(y3_toks) >= 3 else [])
                mm = t_mm(pr[:, :], wb[:, icp * 128:(icp + 1) * 128],
                          hbuf[:, icp * 512:(icp + 1) * 512],
                          deps if icp == 0 else [],
                          start=(icp == 0), stop=(icp == HC - 1))
            wslot_war["b" + str(sl)] = mm
            af = a_op(lambda e, pr=pr, oc=oc: e.activation(
                y3f[:, oc * 512:(oc + 1) * 512], pr[:, :], AF.Copy, scale=s3c(oc)), [mm])
            o = sl * 512
            slot_war = [stat_toks[-2]] if len(stat_toks) >= 2 else []
            extra0 = (G["rows"] + [G["lnstg"]]) if oc == 0 else []
            vb = v_op(lambda e, pr=pr, oc=oc, o=o: e.tensor_scalar(
                bfA[:, o:o + 512], pr[:, :], s3c(oc), None, ALU.mult),
                [mm, af] + slot_war)
            tq = v_op(lambda e, o=o: e.tensor_tensor(
                bfB[:, o:o + 512], bfA[:, o:o + 512], bfA[:, o:o + 512], ALU.mult), [vb])
            m1 = t_mm(sx, ones_bf[:, :], bfA[:, o:o + 512], [vb] + extra0,
                      start=(oc == 0), stop=(oc == 7))
            m2 = t_mm(sq, ones_bf[:, :], bfB[:, o:o + 512], [tq],
                      start=(oc == 0), stop=(oc == 7))
            stat_toks.append(m2)
            y3_toks.append(vb)
        bc = ln_rows(512, sx, sq, BCf0[:, :], BCf1[:, :], m2)
        st_last = None
        for oc in range(8):
            dep = [bc[0], bc[1], G["lnstg"]] + ([st_last] if st_last else [])
            p1 = v_op(lambda e, oc=oc: e.tensor_tensor(
                lnstg1[:, :], y3f[:, oc * 512:(oc + 1) * 512], BCf0[:, :], ALU.mult), dep)
            p2 = v_op(lambda e: e.tensor_tensor(lnstg2[:, :], lnstg1[:, :], BCf1[:, :], ALU.subtract), [p1])
            p3 = g_op(lambda e, oc=oc, t0=t0: e.tensor_tensor(
                res[:, oc * TPC + t0: oc * TPC + t0 + TS],
                res[:, oc * TPC + t0: oc * TPC + t0 + TS], lnstg2[:, :], ALU.add), [p2])
            G["lnstg"] = p2
            st_last = p3
            dma("gpsimd", y_d[oc * 128:(oc + 1) * 128, t0:t0 + TS],
                res[:, oc * TPC + t0: oc * TPC + t0 + TS], [p3], s_out)
            out_count += 1
        prev_tail = bc[1]

    P.add("gpsimd", lambda e: e.memset(den[:, 0:1], 0.0), [(s_out, 16 * out_count)])

    with nc.Block("ffn") as blkF:
        P.flush(blkF)
    ctxF.close()
    st.close()
    return nc


def kernel(L0_attn_x, L0_attn_Wq, L0_attn_Wk, L0_attn_Wv, L0_attn_Wo,
           L0_ffn_w1_w, L0_ffn_w1_s, L0_ffn_w2_w, L0_ffn_w2_s,
           L0_ffn_h1_w, L0_ffn_h1_s, L0_ffn_h2_w, L0_ffn_h2_s,
           L0_ffn_w3_w, L0_ffn_w3_s):
    x = np.asarray(L0_attn_x, dtype=np.float32)
    B, T, _ = x.shape
    c = np.ascontiguousarray

    def blockw(w, icn):
        ogn = w.shape[0] // 128
        return c(w.reshape(ogn, 128, icn, 128).transpose(0, 3, 2, 1))

    shared = {
        "wq": c(np.asarray(L0_attn_Wq, np.float32)),
        "wk": c(np.asarray(L0_attn_Wk, np.float32)),
        "wv": c(np.asarray(L0_attn_Wv, np.float32)),
        "wbo": blockw(np.asarray(L0_attn_Wo, np.float32).T, 8),
        "wb1": blockw(np.asarray(L0_ffn_w1_w, np.float32), 8),
        "wb2": blockw(np.asarray(L0_ffn_w2_w, np.float32), 8),
        "wbh1": blockw(np.asarray(L0_ffn_h1_w, np.float32), 32),
        "wbh2": blockw(np.asarray(L0_ffn_h2_w, np.float32), 32),
        "wbw3": blockw(np.asarray(L0_ffn_w3_w, np.float32), 32),
        "eye_bf": np.eye(128, dtype=ml_dtypes.bfloat16),
        "triu_f": c(np.triu(np.ones((128, 128))).astype(ml_dtypes.bfloat16)),
        "ones_bf": np.ones((128, 1), ml_dtypes.bfloat16),
        "onesr_f": np.ones((1, 128), np.float32),
    }
    scl = np.zeros((128, 136), np.float32)
    scl[:, 0:32] = np.asarray(L0_ffn_w1_s, np.float32).reshape(32, 128).T
    scl[:, 32:64] = np.asarray(L0_ffn_w2_s, np.float32).reshape(32, 128).T
    scl[:, 64:96] = np.asarray(L0_ffn_h1_s, np.float32).reshape(32, 128).T
    scl[:, 96:128] = np.asarray(L0_ffn_h2_s, np.float32).reshape(32, 128).T
    scl[:, 128:136] = np.asarray(L0_ffn_w3_s, np.float32).reshape(8, 128).T
    shared["scl"] = c(scl)

    in_maps = []
    for core in range(8):
        b, q = core // 4, core % 4
        m = dict(shared)
        m["x"] = c(x[b, q * TPC:(q + 1) * TPC, :].T)
        pm = np.zeros((128, 4), np.float32)
        pm[:, :q] = 1.0
        m["pmask"] = pm
        in_maps.append(m)

    nc = build_nc()
    res = run_bass_kernel_spmd(nc, in_maps, core_ids=list(range(8)))
    out = np.empty((B, T, D), np.float32)
    for core in range(8):
        b, q = core // 4, core % 4
        out[b, q * TPC:(q + 1) * TPC, :] = res.results[core]["y"].T
    return out




# version 15
# speedup vs baseline: 1.0645x; 1.0636x over previous
import numpy as np
import ml_dtypes
from contextlib import ExitStack

import concourse.bass as bass
import concourse.mybir as mybir
from concourse.bass_utils import run_bass_kernel_spmd

F32 = mybir.dt.float32
BF16 = mybir.dt.bfloat16
AF = mybir.ActivationFunctionType
ALU = mybir.AluOpType

D = 1024
TPC = 2048
H = 16
HD = 64
CH = 128
NCH = TPC // CH
IC = D // 128
DH = 4096
HC = DH // 128
NSL = 4
TS = TPC // NSL
EPS_ATT = 1e-6
EPS_LN = 1e-5


def soff(h):
    return 260 * ((h // 2) % 2) + 65 * (h // 4)


class Sem:
    def __init__(self, handle, name, by_order=False):
        self.h = handle
        self.name = name
        self.n = 0
        self.by_order = by_order


class Op:
    __slots__ = ("eng", "fn", "waits", "incs")

    def __init__(self, eng, fn, waits, incs):
        self.eng = eng
        self.fn = fn
        self.waits = waits
        self.incs = incs


class Prog:
    def __init__(self):
        self.ops = []

    def add(self, eng, fn, deps=(), inc=None, inck=1):
        waits = [d for d in deps if d is not None]
        incs = []
        tok = None
        if inc is not None:
            inc.n += inck
            incs.append((inc, inck))
            tok = (inc, inc.n)
        self.ops.append(Op(eng, fn, waits, incs))
        return tok

    def flush(self, block):
        per = {}
        for op in self.ops:
            per.setdefault(op.eng, []).append(op)
        table = {"tensor": block.tensor, "vector": block.vector,
                 "scalar": block.scalar, "sync": block.sync, "gpsimd": block.gpsimd}
        for name, ops in per.items():
            def body(eng, ops=ops):
                done = {}
                for op in ops:
                    for s, val in op.waits:
                        if done.get(id(s), -1) >= val:
                            continue
                        eng.wait_ge(s.h, val)
                        done[id(s)] = val
                    instr = op.fn(eng)
                    for s, k in op.incs:
                        instr.then_inc(s.h, k)
                        if s.by_order:
                            done[id(s)] = max(done.get(id(s), -1), s.n)
            table[name](body)
        self.ops = []


def build_nc(stage=3, pb_parts=3, pc=6, dbg=False):
    nc = bass.Bass("TRN2", target_bir_lowering=False)
    st = ExitStack()
    ap = lambda t: t.ap()

    x_d = ap(nc.dram_tensor("x", [D, TPC], F32, kind="ExternalInput"))
    y_d = ap(nc.dram_tensor("y", [D, TPC], F32, kind="ExternalOutput"))
    wq_d = ap(nc.dram_tensor("wq", [D, D], F32, kind="ExternalInput"))
    wk_d = ap(nc.dram_tensor("wk", [D, D], F32, kind="ExternalInput"))
    wv_d = ap(nc.dram_tensor("wv", [D, D], F32, kind="ExternalInput"))
    wbo_d = ap(nc.dram_tensor("wbo", [8, 128, 8, 128], F32, kind="ExternalInput"))
    wb1_d = ap(nc.dram_tensor("wb1", [32, 128, 8, 128], F32, kind="ExternalInput"))
    wb2_d = ap(nc.dram_tensor("wb2", [32, 128, 8, 128], F32, kind="ExternalInput"))
    wbh1_d = ap(nc.dram_tensor("wbh1", [32, 128, 32, 128], F32, kind="ExternalInput"))
    wbh2_d = ap(nc.dram_tensor("wbh2", [32, 128, 32, 128], F32, kind="ExternalInput"))
    wbw3_d = ap(nc.dram_tensor("wbw3", [8, 128, 32, 128], F32, kind="ExternalInput"))
    scl_d = ap(nc.dram_tensor("scl", [128, 136], F32, kind="ExternalInput"))
    eye_d = ap(nc.dram_tensor("eye_bf", [128, 128], BF16, kind="ExternalInput"))
    triu_d = ap(nc.dram_tensor("triu_f", [128, 128], BF16, kind="ExternalInput"))
    onesc_d = ap(nc.dram_tensor("ones_bf", [128, 1], BF16, kind="ExternalInput"))
    onesr_d = ap(nc.dram_tensor("onesr_f", [1, 128], F32, kind="ExternalInput"))
    pmask_d = ap(nc.dram_tensor("pmask", [128, 4], F32, kind="ExternalInput"))

    wscr = {k: ap(nc.dram_tensor(f"wscr_{k}", shp, BF16)) for k, shp in [
        ("wo", [8, 128, 8, 128]), ("w1", [32, 128, 8, 128]), ("w2", [32, 128, 8, 128]),
        ("h1", [32, 128, 32, 128]), ("h2", [32, 128, 32, 128]), ("w3", [8, 128, 32, 128])]}
    wsrc = {"wo": wbo_d, "w1": wb1_d, "w2": wb2_d, "h1": wbh1_d, "h2": wbh2_d, "w3": wbw3_d}
    dbg_d = ap(nc.dram_tensor("dbg", [128, 3072], BF16, kind="ExternalOutput")) if dbg else None
    cc_in = ap(nc.dram_tensor("cc_in", [128, 520], F32))
    cc_out = ap(nc.dram_tensor("cc_out", [512, 520], F32))
    rows_d = ap(nc.dram_tensor("rows_scr", [2, TPC], F32))

    def sem(name, by_order=False):
        return Sem(st.enter_context(nc.semaphore(name)), name, by_order)

    s_pe = sem("s_pe", True)
    s_dve = sem("s_dve")
    s_act = sem("s_act")
    s_pool = sem("s_pool")
    s_x = sem("s_x")
    s_wl = [sem(f"s_wl{i}") for i in range(8)]
    s_cst = {k: sem(f"s_c_{k}") for k in wscr}
    s_cc = sem("s_cc")
    s_ccl = sem("s_ccl")
    s_ccin = sem("s_ccin")
    s_out = sem("s_out")
    s_wb = [sem("s_wb0"), sem("s_wb1")]
    s_rwA = sem("s_rwA")
    s_rwB = sem("s_rwB")

    sb = lambda n, shp, dt: st.enter_context(nc.sbuf_tensor(n, shp, dt))
    res = sb("res", [128, IC * TPC], F32)
    rows1 = sb("rows1", [128, 1024], F32)
    rows2 = sb("rows2", [128, 512], F32)
    bfA = sb("bfA", [128, 1024], BF16)
    bfB = sb("bfB", [128, 1024], BF16)
    stg1 = sb("stg1", [128, 1024], F32)
    stg2 = sb("stg2", [128, 1024], F32)
    lnstg1 = sb("lnstg1", [128, 512], F32)
    lnstg2 = sb("lnstg2", [128, 512], F32)
    scl = sb("scl_sb", [128, 136], F32)
    eye_bf = sb("eye_sb", [128, 128], BF16)
    triu_f = sb("triu_sb", [128, 128], BF16)
    ones_bf = sb("ones_sb", [128, 1], BF16)
    onesr_f = sb("onesr_sb", [1, 128], F32)
    pmask = sb("pmask_sb", [128, 4], F32)
    den = sb("den_sb", [128, 8], F32)
    cst = sb("cst_sb", [128, 4], F32)
    nc.const_aps.register(F32, EPS_LN, cst[:, 0:1])
    nc.const_aps.register(F32, -0.5, cst[:, 1:2])
    nc.const_aps.register(F32, 0.0, cst[:, 2:3])
    nc.const_aps.register(F32, 1.0, cst[:, 3:4])

    s1c = lambda og: scl[:, og:og + 1]
    s2c = lambda og: scl[:, 32 + og:33 + og]
    sh1c = lambda og: scl[:, 64 + og:65 + og]
    sh2c = lambda og: scl[:, 96 + og:97 + og]
    s3c = lambda oc: scl[:, 128 + oc:129 + oc]

    ctxAttn = ExitStack()
    vhat = ctxAttn.enter_context(nc.sbuf_tensor("vhat", [128, NCH * (H * 65)], BF16))
    wsb = ctxAttn.enter_context(nc.sbuf_tensor("wsb", [128, 2 * 8192], BF16))
    ktmc = ctxAttn.enter_context(nc.sbuf_tensor("ktmc", [128, 2 * 1024], BF16))
    ST_A = ctxAttn.enter_context(nc.psum_tensor("ST_A", [128, 512], F32))
    ST_B = ctxAttn.enter_context(nc.psum_tensor("ST_B", [128, 512], F32))

    P = Prog()

    G = {"elu": None, "rows": [], "lnstg": None, "cst": None,
         "elu_i": 0, "elu_s": [None, None]}

    def t_mm(out, lhsT, rhs, deps, start=True, stop=True, tp=None):
        return P.add("tensor",
                     lambda e: e.matmul(out, lhsT, rhs, start=start, stop=stop,
                                        tile_position=tp),
                     deps, inc=s_pe)

    def t_tr(out, in_, deps):
        return P.add("tensor", lambda e: e.transpose(out, in_, eye_bf[:, :]),
                     deps, inc=s_pe)

    def v_op(fn, deps):
        return P.add("vector", fn, deps, inc=s_dve)

    def a_op(fn, deps):
        return P.add("scalar", fn, deps, inc=s_act)

    def g_op(fn, deps):
        return P.add("gpsimd", fn, deps, inc=s_pool)

    def dma(eng, out, in_, deps, semobj):
        return P.add(eng, lambda e: e.dma_start(out=out, in_=in_), deps,
                     inc=semobj, inck=16)

    def elu1(ps, out, dep, W, extra=()):
        sl = G["elu_i"] % 2
        G["elu_i"] += 1
        o = sl * 512
        s1 = stg1[:, o:o + W]
        s2 = stg2[:, o:o + W]
        deps = [dep, G["elu_s"][sl]] + list(extra)
        a1 = a_op(lambda e: e.activation(s1, ps, AF.Relu), deps)
        v1 = v_op(lambda e: e.tensor_tensor(s2, ps, s1, ALU.subtract), [a1])
        a2 = a_op(lambda e: e.activation(s2, s2, AF.Exp), [v1])
        v2 = v_op(lambda e: e.tensor_tensor(out, s1, s2, ALU.add), [a2])
        G["elu_s"][sl] = v2
        return v2

    def ln_rows(W, sx, sq, bc0, bc1, dep_stats, save_g=None):
        mu = rows1[0:1, 0:W]
        tmp = rows1[0:1, 512:512 + W]
        var = rows2[0:1, 0:W]
        r_ = var
        mr = mu
        rw = list(G["rows"]) + ([G["lnstg"]] if G["lnstg"] else [])
        t1 = v_op(lambda e: e.tensor_scalar(mu, sx, 1.0 / D, None, ALU.mult), [dep_stats] + rw)
        t2 = v_op(lambda e: e.tensor_scalar(var, sq, 1.0 / D, None, ALU.mult), [dep_stats] + rw)
        t3 = v_op(lambda e: e.tensor_tensor(tmp, mu, mu, ALU.mult), [t1])
        t4 = v_op(lambda e: e.tensor_tensor(var, var, tmp, ALU.subtract), [t2, t3])
        a1 = a_op(lambda e: e.activation(r_, var, AF.Ln, bias=EPS_LN), [t4, G["cst"]] + rw)
        a2 = a_op(lambda e: e.activation(r_, r_, AF.Exp, scale=-0.5), [a1])
        bc_r = t_mm(bc0, onesr_f[:, :], r_, [a2] + rw)
        t5 = v_op(lambda e: e.tensor_tensor(mr, mu, r_, ALU.mult), [a2])
        bc_m = t_mm(bc1, onesr_f[:, :], mr, [t5])
        G["rows"] = [bc_r, bc_m, t5]
        if save_g is not None:
            t0s = save_g * 512
            sr = dma("scalar", rows_d[0:1, t0s:t0s + 512], r_, [a2], s_rwA)
            sm = dma("scalar", rows_d[1:2, t0s:t0s + 512], mr, [t5], s_rwA)
            G["rows"] += [sr, sm]
        return bc_r, bc_m

    def ln_stats(W, src_f32, sx, sq, extra_dep, sq_dep=None):
        m1 = m2 = None
        prev = []
        for ic in range(IC):
            o = (ic % 2) * 512
            b_x = bfA[:, o:o + W]
            b_q = bfB[:, o:o + W]
            war = [prev[-2]] if len(prev) >= 2 else []
            dep = ([extra_dep] if extra_dep else []) + war
            tc = v_op(lambda e, o=b_x, i=src_f32(ic): e.tensor_copy(o, i), dep)
            tq = v_op(lambda e, o=b_q, i=b_x: e.tensor_tensor(o, i, i, ALU.mult), [tc])
            m1 = t_mm(sx, ones_bf[:, :], b_x, [tc], start=(ic == 0), stop=(ic == IC - 1))
            m2 = t_mm(sq, ones_bf[:, :], b_q,
                      [tq] + ([sq_dep] if (ic == 0 and sq_dep) else []),
                      start=(ic == 0), stop=(ic == IC - 1))
            prev.append(m2)
        return m2, m1

    def ln_apply(W, src_f32, out_bf, bc0, bc1, bc_toks, n=IC):
        toks = []
        for ic in range(n):
            dep = [bc_toks[0], bc_toks[1], G["lnstg"]] + ([toks[-1]] if toks else [])
            p1 = v_op(lambda e, i=src_f32(ic): e.tensor_tensor(lnstg1[:, 0:W], i, bc0, ALU.mult), dep)
            p2 = v_op(lambda e, o=out_bf(ic): e.tensor_tensor(o, lnstg1[:, 0:W], bc1, ALU.subtract), [p1])
            toks.append(p2)
        G["lnstg"] = toks[-1]
        return toks

    ctxA = ExitStack()
    sbA = lambda n, shp, dt: ctxA.enter_context(nc.sbuf_tensor(n, shp, dt))
    psA = lambda n, shp, dt=F32: ctxA.enter_context(nc.psum_tensor(n, shp, dt))
    wstg = sbA("wstg", [128, 2048], F32)
    xlnA = sbA("xlnA", [128, IC * 512], BF16)
    sst = sbA("sst", [128, 520], F32)
    PR0 = psA("PR0", [128, 512])
    PR1 = psA("PR1", [128, 512])
    BC0 = psA("BC0", [128, 512])
    BC1 = psA("BC1", [128, 512])
    STT = psA("STT", [1, 1024])

    c1 = v_op(lambda e: e.memset(cst[:, 0:1], EPS_LN), [])
    c2 = v_op(lambda e: e.memset(cst[:, 1:2], -0.5), [c1])
    c3 = v_op(lambda e: e.memset(cst[:, 2:3], 0.0), [c2])
    G["cst"] = v_op(lambda e: e.memset(cst[:, 3:4], 1.0), [c3])
    for ic in range(IC):
        dma("sync", res[:, ic * TPC:(ic + 1) * TPC], x_d[ic * 128:(ic + 1) * 128, :], [], s_x)
    dma("sync", scl[:, :], scl_d, [], s_x)
    dma("sync", eye_bf[:, :], eye_d, [], s_x)
    dma("sync", triu_f[:, :], triu_d, [], s_x)
    dma("sync", ones_bf[:, :], onesc_d, [], s_x)
    dma("sync", onesr_f[:, :], onesr_d, [], s_x)
    dma("sync", pmask[:, :], pmask_d, [], s_x)
    tok_x = (s_x, s_x.n)

    def load_w_cast(w_dram, slot_base, prev):
        casts = list(prev)
        for ic in range(IC):
            half = (ic % 2) * 1024
            dep = [casts[-2]] if len(casts) >= 2 else []
            ld = dma("sync", wstg[:, half:half + 1024],
                     w_dram[ic * 128:(ic + 1) * 128, :], dep, s_wb[ic % 2])
            c = v_op(lambda e, o=wsb[:, slot_base + ic * 1024: slot_base + (ic + 1) * 1024],
                     i=wstg[:, half:half + 1024]: e.tensor_copy(o, i), [ld])
            casts.append(c)
        return casts[-1], casts[-2:]

    tok_wk, chain = load_w_cast(wk_d, 0, [])
    tok_wv, chain = load_w_cast(wv_d, 8192, chain)

    cast_toks = {}
    first_cast = True
    for name in ["wo", "w1", "w2", "h1", "h2", "w3"]:
        n = {"wo": 8, "w1": 32, "w2": 32, "h1": 32, "h2": 32, "w3": 8}[name]
        for og in range(n):
            deps = [tok_x, tok_wk, tok_wv] if first_cast else []
            first_cast = False
            dma("gpsimd", wscr[name][og], wsrc[name][og], deps, s_cst[name])
        cast_toks[name] = (s_cst[name], s_cst[name].n)

    vhat_ones = vhat.ap().rearrange("p (c h o) -> p (c h) o", c=NCH, h=H)[:, :, 64:65]
    tok_vones = v_op(lambda e: e.memset(vhat_ones, 1.0), [])

    pr_war = {0: None, 1: None}
    state_last = None
    k_ev_last = None
    prev_group_tail = tok_x
    tok_sst0 = v_op(lambda e: e.memset(sst[:, :], 0.0), [])
    sst_add = tok_sst0
    for g in range(4):
        t0 = g * 512
        src = lambda ic, t0=t0: res[:, ic * TPC + t0: ic * TPC + t0 + 512]
        xo = lambda ic: xlnA[:, ic * 512:(ic + 1) * 512]
        stats_dep = prev_group_tail
        m_last, _ = ln_stats(512, src, STT[0:1, 0:512], STT[0:1, 512:1024], stats_dep)
        bc = ln_rows(512, STT[0:1, 0:512], STT[0:1, 512:1024], BC0[:, :], BC1[:, :], m_last,
                     save_g=g)
        ap_toks = ln_apply(512, src, xo, BC0[:, :], BC1[:, :], bc)
        for cc4 in range(4):
            c = g * 4 + cc4
            v_ev = {}
            for eh in range(2):
                pr = PR0 if eh == 0 else PR1
                mmt = None
                for ic in range(IC):
                    lhs = xlnA[:, ic * 512 + cc4 * 128: ic * 512 + cc4 * 128 + 128]
                    rhs = wsb[:, 8192 + ic * 1024 + eh * 512: 8192 + ic * 1024 + eh * 512 + 512]
                    deps = [ap_toks[ic], tok_wv] if ic == 0 else []
                    if ic == 0:
                        deps.append(pr_war[eh])
                    else:
                        deps = [ap_toks[ic]]
                    mmt = t_mm(pr[:, :], lhs, rhs, deps, start=(ic == 0), stop=(ic == IC - 1))
                vv = vhat.ap().rearrange("p (c h o) -> p c h o", c=NCH, h=H)[:, c, eh * 8:(eh + 1) * 8, 0:64]
                prv = pr.ap().rearrange("p (h o) -> p h o", h=8)
                ev = v_op(lambda e, o=vv, i=prv: e.tensor_copy(o, i), [mmt, tok_vones])
                pr_war[eh] = ev
                v_ev[eh] = ev
            kslot = ktmc[:, (c % 2) * 1024:(c % 2) * 1024 + 1024]
            k_ev = {}
            for eh in range(2):
                pr = PR0 if eh == 0 else PR1
                mmt = None
                for ic in range(IC):
                    lhs = xlnA[:, ic * 512 + cc4 * 128: ic * 512 + cc4 * 128 + 128]
                    rhs = wsb[:, ic * 1024 + eh * 512: ic * 1024 + eh * 512 + 512]
                    deps = [ap_toks[ic], tok_wk, pr_war[eh]] if ic == 0 else [ap_toks[ic]]
                    mmt = t_mm(pr[:, :], lhs, rhs, deps, start=(ic == 0), stop=(ic == IC - 1))
                ev = elu1(pr[:, :], kslot[:, eh * 512:eh * 512 + 512], mmt, 512,
                          extra=[state_last])
                pr_war[eh] = ev
                k_ev[eh] = ev
            k_ev_last = k_ev[1]
            for h in range(H):
                lhs = kslot[:, h * 64:(h + 1) * 64]
                rhs = vhat[:, c * (H * 65) + h * 65: c * (H * 65) + (h + 1) * 65]
                stp = ST_A if (h // 2) % 2 == 0 else ST_B
                pbase = 64 * (h % 2)
                slot = h // 4
                out = stp[pbase:pbase + 64, slot * 65:(slot + 1) * 65]
                tp = (0, pbase) if pbase else None
                state_last = t_mm(out, lhs, rhs,
                                  [k_ev[h // 8], v_ev[h // 8], sst_add],
                                  start=True, stop=True, tp=tp)
            vA = bass.AP(sst, 0, [[520, 128], [130, 4], [1, 65]])
            vB = bass.AP(sst, 65, [[520, 128], [130, 4], [1, 65]])
            sA = bass.AP(ST_A, 0, [[512, 128], [65, 4], [1, 65]])
            sB = bass.AP(ST_B, 0, [[512, 128], [65, 4], [1, 65]])
            a1_ = v_op(lambda e: e.tensor_tensor(vA, vA, sA, ALU.add), [state_last])
            sst_add = v_op(lambda e: e.tensor_tensor(vB, vB, sB, ALU.add), [state_last])
        prev_group_tail = state_last

    d1 = dma("sync", cc_in, sst[:, :], [sst_add], s_ccin)
    cc_tok = P.add("gpsimd",
                   lambda e: e.collective_compute(
                       "AllGather", ALU.bypass,
                       replica_groups=[[0, 1, 2, 3], [4, 5, 6, 7]],
                       ins=[cc_in], outs=[cc_out]),
                   [d1], inc=s_cc)

    if stage < 2:
        P.add("gpsimd", lambda e: e.memset(den[:, 0:1], 0.0), [(s_cc, s_cc.n)])
    with nc.Block("pA", no_gpsimd_drain=True) as blkA:
        P.flush(blkA)
    ctxA.close()
    if stage < 2:
        ctxAttn.close()
        st.close()
        return nc

    ctxB = ExitStack()
    sbB = lambda n, shp, dt: ctxB.enter_context(nc.sbuf_tensor(n, shp, dt))
    psB = lambda n, shp, dt=F32: ctxB.enter_context(nc.psum_tensor(n, shp, dt))
    xlnB = sbB("xlnB", [128, 2 * IC * 256], BF16)
    qst = sbB("qst", [128, 2 * 8 * 256], BF16)
    kfm = sbB("kfm", [128, 1024], BF16)
    atm = sbB("atm", [128, 2 * 512], BF16)
    ytm = sbB("ytm", [128, 1024], BF16)
    yfm = sbB("yfm", [128, 2 * 8 * 256], BF16)
    wos = sbB("wos", [128, 2 * 1024], BF16)
    aof = sbB("aof", [128, 8 * 256], F32)
    aob = sbB("aob", [128, 8 * 256], BF16)
    sinit = sbB("sinit", [128, 520], F32)
    ccg = sbB("ccg", [128, 520], F32)
    ssb = sbB("ssb", [128, 520], BF16)

    PS_tr = psB("PS_tr", [128, 1024], BF16)
    PS_at = psB("PS_at", [128, 512])
    PS_y = psB("PS_y", [128, 260])
    PR0b = psB("PR0b", [128, 512])
    PR1b = psB("PR1b", [128, 512])
    PSm = psB("PSm", [128, 512])

    wq_casts = list(chain)
    for ic in range(IC):
        half = (ic % 2) * 1024
        dep = [wq_casts[-2]] if len(wq_casts) >= 2 else []
        ld = dma("sync", aof[:, half:half + 1024], wq_d[ic * 128:(ic + 1) * 128, :], dep, s_wb[ic % 2])
        c = v_op(lambda e, o=wsb[:, 8192 + ic * 1024: 8192 + (ic + 1) * 1024],
                 i=aof[:, half:half + 1024]: e.tensor_copy(o, i), [ld])
        wq_casts.append(c)
    tok_wq = wq_casts[-1]

    t_ms = v_op(lambda e: e.memset(sinit[:, :], 0.0), [])
    acc = t_ms
    for r in range(4):
        ld = dma("sync", ccg[:, :], cc_out[r * 128:(r + 1) * 128, :], [(s_cc, s_cc.n), acc], s_ccl)
        acc = v_op(lambda e, m=pmask[:, r:r + 1]: e.scalar_tensor_tensor(
            sinit[:, :], ccg[:, :], m, sinit[:, :], ALU.mult, ALU.add), [ld, acc])
    tok_sinit = acc

    pr_warB = {0: tok_wq, 1: tok_wq}
    state_lastB = None
    state_addB = None
    ssb_tok = None
    pa_tail = sst_add
    PB = {"psat": None}
    rowld = {}
    kev_pend = {}

    def emit_rowload(g2l, deps):
        if g2l >= 8:
            return
        t0l = g2l * 256
        l1 = dma("sync", rows2[0:1, 256:512], rows_d[0:1, t0l:t0l + 256],
                 deps + [(s_rwA, 16 * 8)], s_rwB)
        l2 = dma("sync", rows1[0:1, 768:1024], rows_d[1:2, t0l:t0l + 256],
                 deps, s_rwB)
        rowld[g2l] = (l1, l2)
    ytr_war = None
    kf_war = None
    y_div_last = None
    ln2_last = None
    wo_war = {0: None, 1: None}
    yfm_war = {0: None, 1: None}
    qst_war = {0: None, 1: None}
    xln_war = {0: None, 1: None}
    aof_war = tok_wq
    k_ev_lastB = None

    for g2 in range(8):
        t0 = g2 * 256
        slot = g2 % 2
        src = lambda ic, t0=t0: res[:, ic * TPC + t0: ic * TPC + t0 + 256]
        xo = lambda ic, slot=slot: xlnB[:, slot * IC * 256 + ic * 256: slot * IC * 256 + (ic + 1) * 256]
        if g2 == 0:
            emit_rowload(0, [])
        bc_deps = [rowld[g2][0], rowld[g2][1], PB["psat"]]
        if g2 == 0:
            bc_deps.append(pa_tail)
        bc_r = t_mm(PS_at[:, 0:256], onesr_f[:, :], rows2[0:1, 256:512], bc_deps)
        bc_m = t_mm(PS_at[:, 256:512], onesr_f[:, :], rows1[0:1, 768:1024], [])
        bc = (bc_r, bc_m)
        ap_toks = ln_apply(256, src, xo, PS_at[:, 0:256], PS_at[:, 256:512], bc)
        xln_done = ap_toks
        emit_rowload(g2 + 1, [bc_r, bc_m])
        q_toks = [None] * 8
        for p in range(4):
            pr = PR0b if p % 2 == 0 else PR1b
            mmt = None
            for half in range(2):
                dc = 2 * p + half
                for ic in range(IC):
                    lhsT = wsb[:, 8192 + ic * 1024 + dc * 128: 8192 + ic * 1024 + dc * 128 + 128]
                    rhs = xlnB[:, slot * IC * 256 + ic * 256: slot * IC * 256 + (ic + 1) * 256]
                    deps = [xln_done[ic]]
                    if ic == 0 and half == 0:
                        deps += [tok_wq, pr_warB[p % 2]]
                    mmt = t_mm(pr[:, half * 256:half * 256 + 256], lhsT, rhs, deps,
                               start=(ic == 0), stop=(ic == IC - 1))
            qo = qst[:, slot * 2048 + 2 * p * 256: slot * 2048 + (2 * p + 2) * 256]
            ev = elu1(pr[:, 0:512], qo, mmt, 512, extra=[qst_war[slot], state_lastB])
            pr_warB[p % 2] = ev
            q_toks[2 * p] = q_toks[2 * p + 1] = ev
        qst_war[slot] = q_toks[-1]
        def emit_kproj(c, xd):
            cil = c % 2
            slotl = (c // 2) % 2
            kslot_l = ktmc[:, (c % 2) * 1024:(c % 2) * 1024 + 1024]
            kev = {}
            for eh in range(2):
                pr = PR0b if eh == 0 else PR1b
                mmt = None
                for ic in range(IC):
                    lhsT = xlnB[:, slotl * IC * 256 + ic * 256 + cil * 128: slotl * IC * 256 + ic * 256 + cil * 128 + 128]
                    rhs = wsb[:, ic * 1024 + eh * 512: ic * 1024 + eh * 512 + 512]
                    deps = [xd[ic]]
                    if ic == 0:
                        deps += [tok_wk, pr_warB[eh]]
                    mmt = t_mm(pr[:, :], lhsT, rhs, deps, start=(ic == 0), stop=(ic == IC - 1))
                ev = elu1(pr[:, :], kslot_l[:, eh * 512:eh * 512 + 512], mmt, 512,
                          extra=[state_lastB])
                pr_warB[eh] = ev
                kev[eh] = ev
            kev_pend[c] = kev

        for ci in (range(2) if pb_parts >= 2 else []):
            c = 2 * g2 + ci
            kslot = ktmc[:, (c % 2) * 1024:(c % 2) * 1024 + 1024]
            if c not in kev_pend:
                emit_kproj(c, xln_done)
            k_ev = kev_pend.pop(c)
            k_ev_lastB = k_ev[1]
            ktr = []
            for j in range(8):
                deps = [k_ev[j // 4]]
                if ytr_war is not None and j == 0:
                    deps.append(ytr_war)
                ktr.append(t_tr(PS_tr[:, j * 128:(j + 1) * 128],
                                kslot[:, j * 128:(j + 1) * 128], deps))
            kf = kfm[:, 0:1024]
            kf_tok0 = v_op(lambda e: e.tensor_copy(kf[:, 0:512], PS_tr[:, 0:512]),
                           ktr[0:4] + [y_div_last])
            kf_tok1 = v_op(lambda e: e.tensor_copy(kf[:, 512:1024], PS_tr[:, 512:1024]),
                           ktr[4:8] + [y_div_last])
            kf_tok = kf_tok1
            if pc < 2:
                ytr_war = kf_tok
                continue
            sdep = [tok_sinit if c == 0 else state_addB]
            ssb_tok = v_op(lambda e: e.tensor_copy(ssb[:, :], sinit[:, :]), sdep)
            if pc < 3:
                ytr_war = ssb_tok
                continue
            waves = [[0, 2, 4, 6], [8, 10, 12, 14], [1, 3, 5, 7], [9, 11, 13, 15]]
            wave_mk = {}
            atm_war = {0: None, 1: None}
            at_war = None

            def emit_at(w):
                nonlocal at_war
                heads = waves[w]
                pb = 64 * (heads[0] % 2)
                at_toks = []
                for j, h in enumerate(heads):
                    lhsT = kf[pb:pb + 64, (h // 2) * 128:(h // 2) * 128 + 128]
                    rhs = qst[pb:pb + 64,
                              slot * 2048 + (h // 2) * 256 + ci * 128: slot * 2048 + (h // 2) * 256 + ci * 128 + 128]
                    deps = [kf_tok0 if w in (0, 2) else kf_tok1, q_toks[h // 2]]
                    if j == 0:
                        deps.append(at_war)
                        deps.append(xln_done[-1])
                    at_toks.append(t_mm(PS_at[:, j * 128:(j + 1) * 128], lhsT, rhs, deps))
                if pc == 36 or (pc == 38 and w == 1):
                    at_war = at_toks[-1]
                    wave_mk[w] = at_toks
                    return
                am = atm[:, (w % 2) * 512:(w % 2) * 512 + 512]
                triu_rep = bass.AP(triu_f, 0, [[128, 128], [0, 4], [1, 128]])
                mk = v_op(lambda e, am=am: e.tensor_tensor(
                    am[:, 0:512], PS_at[:, 0:512], triu_rep, ALU.mult),
                    [at_toks[-1], atm_war[w % 2]])
                mks = [mk] * 4
                wave_mk[w] = mks
                at_war = mk
                PB["psat"] = mk

            def emit_y(w):
                nonlocal y_div_last
                heads = waves[w]
                pb = 64 * (heads[0] % 2)
                am = atm[:, (w % 2) * 512:(w % 2) * 512 + 512]
                mks = wave_mk[w]
                ymm = []
                for j, h in enumerate(heads):
                    m1 = t_mm(PS_y[:, j * 65:(j + 1) * 65],
                              am[:, j * 128:(j + 1) * 128],
                              vhat[:, c * (H * 65) + h * 65: c * (H * 65) + (h + 1) * 65],
                              [mks[j]] + ([y_div_last] if j == 0 else []),
                              start=True, stop=False)
                    m2 = t_mm(PS_y[:, j * 65:(j + 1) * 65],
                              qst[pb:pb + 64, slot * 2048 + (h // 2) * 256 + ci * 128: slot * 2048 + (h // 2) * 256 + ci * 128 + 128],
                              ssb[pb:pb + 64, 65 * (h // 2):65 * (h // 2) + 65],
                              [ssb_tok], start=False, stop=True)
                    ymm.append(m2)
                atm_war[w % 2] = ymm[-1]
                dn = bass.AP(PS_y, 64, [[260, 128], [65, 4]])
                d0 = v_op(lambda e: e.tensor_scalar(den[:, 0:4], dn, EPS_ATT, None, ALU.add), ymm)
                d1_ = v_op(lambda e: e.reciprocal(den[:, 4:8], den[:, 0:4]), [d0])
                for j, h in enumerate(heads):
                    y_div_last = v_op(lambda e, j=j, h=h: e.tensor_scalar(
                        ytm[:, h * 64:(h + 1) * 64],
                        PS_y[:, j * 65:j * 65 + 64], den[:, 4 + j:5 + j], None, ALU.mult), [d1_])

            def pe_fence():
                v = s_pe.n
                P.add("tensor", lambda e, v=v: e.wait_ge(s_pe.h, v), [])

            def dbg_dump(src_ap, off, deps, eng="sync"):
                n = src_ap.free_size()
                sem_ = s_out if eng == "sync" else s_cst["wo"]
                return dma(eng, bass.AP(dbg_d.tensor, off, [[3072, 128], [1, n]]),
                           src_ap, deps, sem_)

            emit_at(0)
            if pc not in (36, 37):
                emit_at(1)
            if pc >= 5 and pc not in (36, 37, 38):
                emit_y(0)
                emit_y(1)
            if dbg and c == 0:
                t1 = dbg_dump(atm[:, 0:512], 0, [wave_mk[0][-1], atm_war[0]])
                t2 = dbg_dump(atm[:, 512:1024], 512, [wave_mk[1][-1], atm_war[1]])
                atm_war[0] = t1
                atm_war[1] = t2
            if pc not in (35, 36, 37, 38):
                pe_fence()
                emit_at(2)
                emit_at(3)
                pe_fence()
                if pc >= 5:
                    emit_y(2)
                    emit_y(3)
            if pc < 5:
                y_div_last = at_war
            if pc < 6:
                state_addB = y_div_last
                ytr_war = y_div_last
                continue
            if dbg and c == 0:
                dbg_ytm = dbg_dump(ytm[:, 0:1024], 1024, [y_div_last])
                dbg_dump(den[:, 0:8], 2048, [y_div_last], eng="gpsimd")
                dbg_dump(ssb[:, 0:520], 2100, [ssb_tok])
            for h in range(H):
                lhsT = kslot[:, h * 64:(h + 1) * 64]
                rhs = vhat[:, c * (H * 65) + h * 65: c * (H * 65) + (h + 1) * 65]
                stp = ST_A if (h // 2) % 2 == 0 else ST_B
                pbase = 64 * (h % 2)
                slot_h = h // 4
                out = stp[pbase:pbase + 64, slot_h * 65:(slot_h + 1) * 65]
                tp = (0, pbase) if pbase else None
                state_lastB = t_mm(out, lhsT, rhs,
                                   [ssb_tok, k_ev[h // 8], state_addB],
                                   start=True, stop=True, tp=tp)
            if c < NCH - 1:
                vA = bass.AP(sinit, 0, [[520, 128], [130, 4], [1, 65]])
                vB = bass.AP(sinit, 65, [[520, 128], [130, 4], [1, 65]])
                sA = bass.AP(ST_A, 0, [[512, 128], [65, 4], [1, 65]])
                sB = bass.AP(ST_B, 0, [[512, 128], [65, 4], [1, 65]])
                v_op(lambda e, vA=vA, sA=sA: e.tensor_tensor(vA, vA, sA, ALU.add), [state_lastB])
                state_addB = v_op(lambda e, vB=vB, sB=sB: e.tensor_tensor(vB, vB, sB, ALU.add), [state_lastB])
            ytr = []
            for j in range(8):
                deps = [y_div_last, kf_tok] if j == 0 else [y_div_last]
                ytr.append(t_tr(PS_tr[:, j * 128:(j + 1) * 128],
                                ytm[:, j * 128:(j + 1) * 128],
                                deps))
            yf_dst = bass.AP(yfm, slot * 2048 + ci * 128, [[4096, 128], [256, 8], [1, 128]])
            ptr_src = PS_tr.ap().rearrange("p (j t) -> p j t", j=8)
            ytr_war = v_op(lambda e, o=yf_dst, i=ptr_src: e.tensor_copy(o, i),
                           ytr + [yfm_war[slot]])
            yfm_done = ytr_war
        if pb_parts < 3:
            continue
        wo_evac = []
        wo_mm_last = None
        for oc in range(8):
            wslot = wos[:, (oc % 2) * 1024:(oc % 2) * 1024 + 1024]
            dep_w = [cast_toks["wo"]] + ([wo_evac[-2]] if len(wo_evac) >= 2 else [wo_war[oc % 2]])
            ld = dma("sync", wslot, wscr["wo"][oc].rearrange("p a b -> p (a b)"), dep_w, s_wl[oc % 2])
            pr = PR0b if oc % 2 == 0 else PR1b
            mmt = None
            for ec in range(8):
                deps = [ld, yfm_done] if ec == 0 else []
                if ec == 0:
                    deps.append(pr_warB[oc % 2])
                mmt = t_mm(pr[:, 0:256], wslot[:, ec * 128:(ec + 1) * 128],
                           yfm[:, slot * 2048 + ec * 256: slot * 2048 + (ec + 1) * 256],
                           deps, start=(ec == 0), stop=(ec == 7))
            wo_mm_last = mmt
            a1 = a_op(lambda e, oc=oc, pr=pr: e.activation(
                aof[:, oc * 256:(oc + 1) * 256], pr[:, 0:256], AF.Copy),
                [mmt, aof_war])
            v1 = v_op(lambda e, oc=oc, pr=pr: e.tensor_copy(
                aob[:, oc * 256:(oc + 1) * 256], pr[:, 0:256]), [mmt, a1])
            pr_warB[oc % 2] = v1
            wo_evac.append(v1)
        yfm_war[slot] = wo_mm_last
        wo_war[0] = wo_evac[-2]
        wo_war[1] = wo_evac[-1]
        sx = PSm[0:1, 0:256]
        sq = PR0b[0:1, 256:512]
        m1 = m2 = None
        m2s = []
        for oc in range(8):
            o = (oc % 2) * 512
            war = [m2s[-2]] if len(m2s) >= 2 else []
            tq = v_op(lambda e, oc=oc, o=o: e.tensor_tensor(
                bfB[:, o:o + 256], aob[:, oc * 256:(oc + 1) * 256],
                aob[:, oc * 256:(oc + 1) * 256], ALU.mult), [wo_evac[oc]] + war)
            deps1 = [wo_evac[oc]] + ((G["rows"] + [G["lnstg"]]) if oc == 0 else [])
            m1 = t_mm(sx, ones_bf[:, :], aob[:, oc * 256:(oc + 1) * 256], deps1,
                      start=(oc == 0), stop=(oc == 7))
            m2 = t_mm(sq, ones_bf[:, :], bfB[:, o:o + 256],
                      [tq] + ([wo_evac[-2]] if oc == 0 else []),
                      start=(oc == 0), stop=(oc == 7))
            m2s.append(m2)
        ln2_m2_last = m2
        bc = ln_rows(256, sx, sq, PSm[:, 0:256], PSm[:, 256:512], m2)
        for oc in range(8):
            dep = [bc[0], bc[1], G["lnstg"]] + ([ln2_last] if ln2_last else [])
            p1 = v_op(lambda e, oc=oc: e.tensor_tensor(
                lnstg1[:, 0:256], aof[:, oc * 256:(oc + 1) * 256], PSm[:, 0:256], ALU.mult), dep)
            p2 = v_op(lambda e: e.tensor_tensor(lnstg2[:, 0:256], lnstg1[:, 0:256],
                                                PSm[:, 256:512], ALU.subtract), [p1])
            p3 = g_op(lambda e, oc=oc, t0=t0: e.tensor_tensor(
                res[:, oc * TPC + t0: oc * TPC + t0 + 256],
                res[:, oc * TPC + t0: oc * TPC + t0 + 256], lnstg2[:, 0:256], ALU.add), [p2])
            G["lnstg"] = p2
            ln2_last = p3
        aof_war = ln2_last

    if stage < 3:
        tail = ln2_last if pb_parts >= 3 else (
            (state_addB if state_addB else ytr_war) if pb_parts >= 2 else qst_war[1])
        P.add("gpsimd", lambda e: e.memset(den[:, 0:1], 0.0),
              [tail, (s_pe, s_pe.n), (s_dve, s_dve.n), (s_act, s_act.n)])
    with nc.Block("pB", no_gpsimd_drain=True) as blkB:
        P.flush(blkB)
    ctxB.close()
    ctxAttn.close()
    if stage < 3:
        st.close()
        return nc

    ctxF = ExitStack()
    sbF = lambda n, shp, dt: ctxF.enter_context(nc.sbuf_tensor(n, shp, dt))
    psF = lambda n, shp, dt=F32: ctxF.enter_context(nc.psum_tensor(n, shp, dt))
    hbuf = sbF("hbuf", [128, HC * 512], BF16)
    gbuf = sbF("gbuf", [128, HC * 512], BF16)
    xln3 = sbF("xln3", [128, IC * 512], BF16)
    wbig = sbF("wbig", [128, 2 * 4096], BF16)
    wsml = sbF("wsml", [128, 4 * 1024], BF16)
    silu_s = sbF("silu_s", [128, 512], BF16)
    y3f = sbF("y3f", [128, 8 * 512], F32)

    PRf = [psF(f"PRf{i}", [128, 512]) for i in range(4)]
    PSmf = psF("PSmf", [1, 1024])
    BCf0 = psF("BCf0", [128, 512])
    BCf1 = psF("BCf1", [128, 512])

    out_count = 0
    prev_tail = ln2_last
    wslot_war = {k: None for k in ["s0", "s1", "s2", "s3", "b0", "b1"]}
    for s in range(NSL):
        t0 = s * TS
        src = lambda ic, t0=t0: res[:, ic * TPC + t0: ic * TPC + t0 + TS]
        xo = lambda ic: xln3[:, ic * 512:(ic + 1) * 512]
        m_last, _ = ln_stats(512, src, PSmf[0:1, 0:512], PSmf[0:1, 512:1024], prev_tail)
        bc = ln_rows(512, PSmf[0:1, 0:512], PSmf[0:1, 512:1024], PRf[2][:, :], PRf[3][:, :], m_last)
        xln_done = ln_apply(512, src, xo, PRf[2][:, :], PRf[3][:, :], bc)
        h_toks = []
        for og in range(HC):
            sl = og % 2
            w1b = wsml[:, sl * 1024: sl * 1024 + 1024]
            w2b = wsml[:, 2048 + sl * 1024: 2048 + sl * 1024 + 1024]
            pr_war_ = [h_toks[-2]] if len(h_toks) >= 2 else [xln_done[-1]]
            ld_eng = "gpsimd" if (s == 0 and og < 2) else "sync"
            l1 = dma(ld_eng, w1b, wscr["w1"][og].rearrange("p a b -> p (a b)"),
                     [cast_toks["w1"], wslot_war["s" + str(sl)]], s_wl[2 + sl])
            l2 = dma(ld_eng, w2b, wscr["w2"][og].rearrange("p a b -> p (a b)"),
                     [cast_toks["w2"], wslot_war["s" + str(2 + sl)]], s_wl[4 + sl])
            pra = PRf[sl * 2]
            prb = PRf[sl * 2 + 1]
            ma = mb = None
            for ic in range(IC):
                deps = ([l1, xln_done[ic]] + pr_war_) if ic == 0 else [xln_done[ic]]
                ma = t_mm(pra[:, :], w1b[:, ic * 128:(ic + 1) * 128], xo(ic),
                          deps, start=(ic == 0), stop=(ic == IC - 1))
            wslot_war["s" + str(sl)] = ma
            sa = a_op(lambda e, pra=pra, og=og: e.activation(
                silu_s[:, :], pra[:, :], AF.Silu, scale=s1c(og)),
                [ma, G["elu"]] + ([h_toks[-1]] if h_toks else []))
            G["elu"] = sa
            for ic in range(IC):
                deps = ([l2, xln_done[ic]] + pr_war_) if ic == 0 else [xln_done[ic]]
                mb = t_mm(prb[:, :], w2b[:, ic * 128:(ic + 1) * 128], xo(ic),
                          deps, start=(ic == 0), stop=(ic == IC - 1))
            wslot_war["s" + str(2 + sl)] = mb
            hv = v_op(lambda e, prb=prb, og=og: e.scalar_tensor_tensor(
                hbuf[:, og * 512:(og + 1) * 512], prb[:, :], s2c(og), silu_s[:, :],
                ALU.mult, ALU.mult), [mb, sa])
            h_toks.append(hv)
        g_toks = []
        for og in range(HC):
            sl = og % 2
            wb = wbig[:, sl * 4096: sl * 4096 + 4096]
            ld_eng = "gpsimd" if (s == 0 and og < 2) else "sync"
            ld = dma(ld_eng, wb, wscr["h1"][og].rearrange("p a b -> p (a b)"),
                     [cast_toks["h1"], wslot_war["b" + str(sl)]], s_wl[6 + sl])
            pr = PRf[og % 3]
            mm = None
            for icp in range(HC):
                deps = [ld, h_toks[-1]] + ([g_toks[-3]] if len(g_toks) >= 3 else [])
                mm = t_mm(pr[:, :], wb[:, icp * 128:(icp + 1) * 128],
                          hbuf[:, icp * 512:(icp + 1) * 512],
                          deps if icp == 0 else [],
                          start=(icp == 0), stop=(icp == HC - 1))
            wslot_war["b" + str(sl)] = mm
            gv = a_op(lambda e, pr=pr, og=og: e.activation(
                gbuf[:, og * 512:(og + 1) * 512], pr[:, :], AF.Gelu, scale=sh1c(og)), [mm])
            g_toks.append(gv)
        u_toks = []
        for og in range(HC):
            sl = og % 2
            wb = wbig[:, sl * 4096: sl * 4096 + 4096]
            ld = dma("sync", wb, wscr["h2"][og].rearrange("p a b -> p (a b)"),
                     [cast_toks["h2"], wslot_war["b" + str(sl)]], s_wl[6 + sl])
            pr = PRf[og % 3]
            mm = None
            for icp in range(HC):
                deps = [ld, g_toks[-1]] + ([u_toks[-3]] if len(u_toks) >= 3 else [])
                mm = t_mm(pr[:, :], wb[:, icp * 128:(icp + 1) * 128],
                          gbuf[:, icp * 512:(icp + 1) * 512],
                          deps if icp == 0 else [],
                          start=(icp == 0), stop=(icp == HC - 1))
            wslot_war["b" + str(sl)] = mm
            uv = a_op(lambda e, pr=pr, og=og: e.activation(
                hbuf[:, og * 512:(og + 1) * 512], pr[:, :], AF.Gelu, scale=sh2c(og)), [mm])
            u_toks.append(uv)
        y3_toks = []
        stat_toks = []
        sx = PSmf[0:1, 0:512]
        sq = PSmf[0:1, 512:1024]
        m1 = m2 = None
        for oc in range(8):
            sl = oc % 2
            wb = wbig[:, sl * 4096: sl * 4096 + 4096]
            ld = dma("sync", wb, wscr["w3"][oc].rearrange("p a b -> p (a b)"),
                     [cast_toks["w3"], wslot_war["b" + str(sl)]], s_wl[6 + sl])
            pr = PRf[oc % 3]
            mm = None
            for icp in range(HC):
                deps = [ld, u_toks[-1]] + ([y3_toks[-3]] if len(y3_toks) >= 3 else [])
                mm = t_mm(pr[:, :], wb[:, icp * 128:(icp + 1) * 128],
                          hbuf[:, icp * 512:(icp + 1) * 512],
                          deps if icp == 0 else [],
                          start=(icp == 0), stop=(icp == HC - 1))
            wslot_war["b" + str(sl)] = mm
            af = a_op(lambda e, pr=pr, oc=oc: e.activation(
                y3f[:, oc * 512:(oc + 1) * 512], pr[:, :], AF.Copy, scale=s3c(oc)), [mm])
            o = sl * 512
            slot_war = [stat_toks[-2]] if len(stat_toks) >= 2 else []
            extra0 = (G["rows"] + [G["lnstg"]]) if oc == 0 else []
            vb = v_op(lambda e, pr=pr, oc=oc, o=o: e.tensor_scalar(
                bfA[:, o:o + 512], pr[:, :], s3c(oc), None, ALU.mult),
                [mm, af] + slot_war)
            tq = v_op(lambda e, o=o: e.tensor_tensor(
                bfB[:, o:o + 512], bfA[:, o:o + 512], bfA[:, o:o + 512], ALU.mult), [vb])
            m1 = t_mm(sx, ones_bf[:, :], bfA[:, o:o + 512], [vb] + extra0,
                      start=(oc == 0), stop=(oc == 7))
            m2 = t_mm(sq, ones_bf[:, :], bfB[:, o:o + 512], [tq],
                      start=(oc == 0), stop=(oc == 7))
            stat_toks.append(m2)
            y3_toks.append(vb)
        bc = ln_rows(512, sx, sq, BCf0[:, :], BCf1[:, :], m2)
        st_last = None
        for oc in range(8):
            dep = [bc[0], bc[1], G["lnstg"]] + ([st_last] if st_last else [])
            p1 = v_op(lambda e, oc=oc: e.tensor_tensor(
                lnstg1[:, :], y3f[:, oc * 512:(oc + 1) * 512], BCf0[:, :], ALU.mult), dep)
            p2 = v_op(lambda e: e.tensor_tensor(lnstg2[:, :], lnstg1[:, :], BCf1[:, :], ALU.subtract), [p1])
            p3 = g_op(lambda e, oc=oc, t0=t0: e.tensor_tensor(
                res[:, oc * TPC + t0: oc * TPC + t0 + TS],
                res[:, oc * TPC + t0: oc * TPC + t0 + TS], lnstg2[:, :], ALU.add), [p2])
            G["lnstg"] = p2
            st_last = p3
            dma("gpsimd", y_d[oc * 128:(oc + 1) * 128, t0:t0 + TS],
                res[:, oc * TPC + t0: oc * TPC + t0 + TS], [p3], s_out)
            out_count += 1
        prev_tail = bc[1]

    P.add("gpsimd", lambda e: e.memset(den[:, 0:1], 0.0), [(s_out, 16 * out_count)])

    with nc.Block("ffn") as blkF:
        P.flush(blkF)
    ctxF.close()
    st.close()
    return nc


def kernel(L0_attn_x, L0_attn_Wq, L0_attn_Wk, L0_attn_Wv, L0_attn_Wo,
           L0_ffn_w1_w, L0_ffn_w1_s, L0_ffn_w2_w, L0_ffn_w2_s,
           L0_ffn_h1_w, L0_ffn_h1_s, L0_ffn_h2_w, L0_ffn_h2_s,
           L0_ffn_w3_w, L0_ffn_w3_s):
    x = np.asarray(L0_attn_x, dtype=np.float32)
    B, T, _ = x.shape
    c = np.ascontiguousarray

    def blockw(w, icn):
        ogn = w.shape[0] // 128
        return c(w.reshape(ogn, 128, icn, 128).transpose(0, 3, 2, 1))

    shared = {
        "wq": c(np.asarray(L0_attn_Wq, np.float32)),
        "wk": c(np.asarray(L0_attn_Wk, np.float32)),
        "wv": c(np.asarray(L0_attn_Wv, np.float32)),
        "wbo": blockw(np.asarray(L0_attn_Wo, np.float32).T, 8),
        "wb1": blockw(np.asarray(L0_ffn_w1_w, np.float32), 8),
        "wb2": blockw(np.asarray(L0_ffn_w2_w, np.float32), 8),
        "wbh1": blockw(np.asarray(L0_ffn_h1_w, np.float32), 32),
        "wbh2": blockw(np.asarray(L0_ffn_h2_w, np.float32), 32),
        "wbw3": blockw(np.asarray(L0_ffn_w3_w, np.float32), 32),
        "eye_bf": np.eye(128, dtype=ml_dtypes.bfloat16),
        "triu_f": c(np.triu(np.ones((128, 128))).astype(ml_dtypes.bfloat16)),
        "ones_bf": np.ones((128, 1), ml_dtypes.bfloat16),
        "onesr_f": np.ones((1, 128), np.float32),
    }
    scl = np.zeros((128, 136), np.float32)
    scl[:, 0:32] = np.asarray(L0_ffn_w1_s, np.float32).reshape(32, 128).T
    scl[:, 32:64] = np.asarray(L0_ffn_w2_s, np.float32).reshape(32, 128).T
    scl[:, 64:96] = np.asarray(L0_ffn_h1_s, np.float32).reshape(32, 128).T
    scl[:, 96:128] = np.asarray(L0_ffn_h2_s, np.float32).reshape(32, 128).T
    scl[:, 128:136] = np.asarray(L0_ffn_w3_s, np.float32).reshape(8, 128).T
    shared["scl"] = c(scl)

    in_maps = []
    for core in range(8):
        b, q = core // 4, core % 4
        m = dict(shared)
        m["x"] = c(x[b, q * TPC:(q + 1) * TPC, :].T)
        pm = np.zeros((128, 4), np.float32)
        pm[:, :q] = 1.0
        m["pmask"] = pm
        in_maps.append(m)

    nc = build_nc()
    res = run_bass_kernel_spmd(nc, in_maps, core_ids=list(range(8)))
    out = np.empty((B, T, D), np.float32)
    for core in range(8):
        b, q = core // 4, core % 4
        out[b, q * TPC:(q + 1) * TPC, :] = res.results[core]["y"].T
    return out


